# version 08
# baseline (speedup 1.0000x reference)
import math
from contextlib import ExitStack

import numpy as np
import ml_dtypes
import concourse.bass as bass
import concourse.mybir as mybir
from concourse.bass_utils import run_bass_kernel_spmd

F32 = mybir.dt.float32
BF16 = mybir.dt.bfloat16
I32 = mybir.dt.int32
ALU = mybir.AluOpType
AF = mybir.ActivationFunctionType
AX = mybir.AxisListType

T = 2048
NQ = 4
QW = 512
CH = 8
NSEQ = 2
H0, HS0, DHR = 16, 64, 32
SCALE0 = 1.0 / math.sqrt(HS0 + DHR)
H1, KV1, DV1, DLAT = 8, 2, 128, 32
SCALE1 = 1.0 / math.sqrt(DV1)
FFH = 3072
FFC = FFH // 128
EPS = 1e-5
MAGIC = 12582912.0
TWO_PI = 2 * math.pi
PI = math.pi
HI2PI = float(np.float32(6.28125))
LO2PI = float(np.float32(TWO_PI) - np.float32(6.28125))


def pack_w(W, mw=128):
    W = np.ascontiguousarray(W)
    K, M = W.shape
    assert M % mw == 0
    nb = M // mw
    if K >= 128:
        assert K % 128 == 0
        kc = K // 128
        Wp = W.reshape(kc, 128, nb, mw).transpose(2, 1, 0, 3).reshape(nb, 128, kc * mw)
    else:
        Wp = W.reshape(1, K, nb, mw).transpose(2, 1, 0, 3).reshape(nb, K, mw)
    return np.ascontiguousarray(Wp)


def prep_host(inputs):
    d = {}
    d["wdq"] = pack_w(np.asarray(inputs["L0_attn_W_DQ"]))
    d["wuq"] = pack_w(np.asarray(inputs["L0_attn_W_UQ"]))
    d["wqr"] = pack_w(np.asarray(inputs["L0_attn_W_QR"]))
    d["wdkv"] = pack_w(np.asarray(inputs["L0_attn_W_DKV"]))
    d["wuk"] = pack_w(np.asarray(inputs["L0_attn_W_UK"]))
    d["wuv"] = pack_w(np.asarray(inputs["L0_attn_W_UV"]))
    d["wkr"] = pack_w(np.asarray(inputs["L0_attn_W_KR"]), mw=32)
    d["wo0"] = pack_w(np.asarray(inputs["L0_attn_W_O"]))
    d["wg"] = pack_w(np.ascontiguousarray(np.asarray(inputs["L0_ffn_w_gate"]).T))
    d["wu"] = pack_w(np.ascontiguousarray(np.asarray(inputs["L0_ffn_w_up"]).T))
    d["wd"] = pack_w(np.ascontiguousarray(np.asarray(inputs["L0_ffn_w_down"]).T))
    d["wq1"] = pack_w(np.asarray(inputs["L1_attn_wq"]))
    d["wkd"] = pack_w(np.asarray(inputs["L1_attn_wk_down"]), mw=64)
    d["wvd"] = pack_w(np.asarray(inputs["L1_attn_wv_down"]), mw=64)
    d["wku"] = pack_w(np.asarray(inputs["L1_attn_wk_up"]))
    d["wvu"] = pack_w(np.asarray(inputs["L1_attn_wv_up"]))
    d["wo1"] = pack_w(np.asarray(inputs["L1_attn_wo"]))
    d["kern"] = pack_w(np.asarray(inputs["L1_ffn_kernel"]))
    d["bias1"] = np.ascontiguousarray(np.asarray(inputs["L1_ffn_bias"]).reshape(8, 128).T)
    d["la"] = pack_w(np.asarray(inputs["L1_ffn_a"]), mw=16)
    d["lb"] = pack_w(np.asarray(inputs["L1_ffn_b"]))
    x_fm = np.ascontiguousarray(np.asarray(inputs["L0_attn_x"]).transpose(0, 2, 1))
    return d, x_fm


class Sem:
    def __init__(self, nc, name):
        self.h = nc.alloc_semaphore(name)
        self.n = 0

    def bump(self, amt):
        self.n += amt
        return self.n


class Trk:
    def __init__(self):
        self.w = {}
        self.r = {}

    def rwait(self):
        return list(self.w.items())

    def wwait(self):
        return list(self.w.items()) + list(self.r.items())

    def setw(self, tok):
        self.w[tok[0]] = max(self.w.get(tok[0], 0), tok[1])

    def addr(self, tok):
        self.r[tok[0]] = max(self.r.get(tok[0], 0), tok[1])


ENGS = ("sync", "scalar", "tensor", "vector", "gpsimd")


class Kb:
    def __init__(self, nc):
        self.nc = nc
        self.items = {e: [] for e in ENGS}
        self.esem = {e: Sem(nc, f"ms_{e}") for e in ENGS}
        self.nsteps = 0

    def sem(self, name):
        return Sem(self.nc, name)

    def step(self, eng, fn, reads=(), writes=(), waits=(), sem="auto", amt=1):
        s = self.esem[eng] if sem == "auto" else sem
        tgt = s.bump(amt)
        tok = (s, tgt)
        allw = list(waits) + list(getattr(self, "global_waits", []))
        for t in reads:
            allw += t.rwait()
        for t in writes:
            allw += t.wwait()
        for t in reads:
            t.addr(tok)
        for t in writes:
            t.setw(tok)
        wl = [(ss.h, v) for ss, v in allw if v > 0]
        self.items[eng].append((wl, fn, s.h, amt))
        self.nsteps += 1
        return tok

    def barrier_token(self):
        return [(s, s.n) for s in self.esem.values()]

    def finalize(self, block):
        own = {e: self.esem[e].h for e in ENGS}
        for e in ENGS:
            items = self.items[e]
            ownh = own[e]
            candrain = e in ("vector", "scalar", "gpsimd")

            def body(eng, items=items, ownh=ownh, candrain=candrain):
                waited = {}
                hist = [0, 0, 0]
                for wl, fn, semh, amt in items:
                    if candrain and any(h is ownh and v > hist[-3] for h, v in wl):
                        eng.drain()
                    for h, v in wl:
                        if h is ownh:
                            continue
                        if waited.get(h, -1) >= v:
                            continue
                        waited[h] = v
                        eng.wait_ge(h, v)
                    ins = fn(eng)
                    ins.then_inc(semh, amt)
                    hist.append(hist[-1] + (amt if semh is ownh else 0))
            getattr(block, e)(body)


class WStream:
    def __init__(self, kb, name, src_ap, nb, P, F, sbuf_at, batch=1, nbuf=2,
                 share=None):
        self.kb = kb
        self.name = name
        self.src = src_ap
        self.nb, self.P, self.F, self.batch = nb, P, F, batch
        nc = kb.nc
        self.stash = nc.dram_tensor(f"st_{name}", [nb, P, F], BF16, kind="Internal").ap()
        self.nbuf = nbuf
        self.sems = [kb.sem(f"wl_{name}{i}") for i in range(nbuf)]
        if share is not None:
            self.bufs = share.bufs
            self.trks = share.trks
        else:
            self.bufs = []
            self.trks = []
            for i in range(nbuf):
                t = nc.alloc_sbuf_tensor_at(f"wb_{name}{i}", [P, batch * F], BF16,
                                            offset=sbuf_at + i * batch * F * 2)
                self.bufs.append(t)
                self.trks.append(Trk())
        self.stash_tok = {}
        self.cnt = 0

    def load(self, bi):
        kb = self.kb
        slot = self.cnt % self.nbuf
        self.cnt += 1
        buf, trk, sem = self.bufs[slot], self.trks[slot], self.sems[slot]
        g = min(self.batch, self.nb - bi * self.batch)
        first = bi not in self.stash_tok
        bufv = buf[:, 0:g * self.F].rearrange("p (g f) -> p g f", g=g)
        if first:
            src = self.src[bi * self.batch: bi * self.batch + g].rearrange(
                "g p f -> p g f")
            kb.step("gpsimd", lambda e, d=bufv, s=src: e.dma_start(d, s),
                    writes=[trk], sem=sem, amt=16)
            sdst = self.stash[bi * self.batch: bi * self.batch + g].rearrange(
                "g p f -> p g f")
            tok = kb.step("gpsimd", lambda e, d=sdst, s=bufv: e.dma_start(d, s),
                          reads=[trk], waits=[(sem, sem.n)], sem=sem, amt=16)
            self.stash_tok[bi] = tok
        else:
            stok = self.stash_tok[bi]
            src = self.stash[bi * self.batch: bi * self.batch + g].rearrange(
                "g p f -> p g f")
            kb.step("sync", lambda e, d=bufv, s=src: e.dma_start(d, s),
                    writes=[trk], waits=[stok], sem=sem, amt=16)
        return (buf, trk, (sem, sem.n))

    def tile(self, h, blk_in_batch, kc, mw):
        buf = h[0]
        return buf[:, blk_in_batch * self.F + kc * mw:(blk_in_batch * self.F) + kc * mw + mw]


def ceil_div(a, b):
    return (a + b - 1) // b


def ycol0(h):
    return (h // 7) * 512 + (h % 7) * 65


class Core:
    def __init__(self, nc, kb):
        self.nc = nc
        self.kb = kb
        self.MBASE = 16512
        self.cur = self.MBASE
        self.trks = {}

    def at(self, name, shape, dtype, off):
        t = self.nc.alloc_sbuf_tensor_at(name, shape, dtype, offset=self.MBASE + off)
        self.trks[t.name] = Trk()
        return t

    def trk(self, t):
        return self.trks[t.name if hasattr(t, "name") else t]

    def pe_flush(self, reads=()):
        return self.kb.step("tensor", lambda e: e.matmul(
            self.ps_c[0:1, 48:50], self.ones_bf[0:1, 0:1], self.ones_bf[0:1, 0:2],
            start=True, stop=True, skip_group_check=True), reads=list(reads),
            writes=[self.trk(self.ps_c)])

    def dump(self, name, ap, trks):
        dbg = getattr(self, "dbg", False)
        if not dbg or name in getattr(self, "dumped", {}):
            return
        if isinstance(dbg, (set, frozenset, list, tuple)) and name not in dbg:
            return
        self.dumped = getattr(self, "dumped", {})
        self.dumped[name] = True
        d = self.nc.dram_tensor(f"dbg_{name}", list(ap.shape), ap.dtype,
                                kind="ExternalOutput").ap()
        s = self.kb.sem(f"dbg_{name}")
        self.kb.step("sync", lambda e: e.dma_start(d, ap), reads=list(trks),
                     sem=s, amt=16)
        self.kb.dbg_sems = getattr(self.kb, "dbg_sems", [])
        self.kb.dbg_sems.append((s, s.n))


def setup_consts(core):
    nc, kb = core.nc, core.kb
    c = core
    c.rope4 = c.at("rope4", [128, 2048], F32, 0)
    c.ident = c.at("ident", [128, 128], BF16, 8192)
    c.ones_bf = c.at("ones_bf", [128, 128], BF16, 8448)
    c.epsc = c.at("epsc", [128, 8], F32, 8704)
    c.rows_a = c.at("rows_a", [1, 512], F32, 8736)
    c.rows_b = c.at("rows_b", [1, 512], F32, 10784)
    c.rows_rstd = c.at("rows_rstd", [1, 512], BF16, 12832)
    c.rows_mr = c.at("rows_mr", [1, 512], BF16, 13856)
    c.qn2 = c.at("qn2", [16, 512], F32, 14880)
    c.kmax = c.at("kmax", [1, 32], F32, 16928)
    c.bias3 = c.at("bias3", [128, 48], F32, 17056)
    c.recips = c.at("recips", [128, 16], F32, 17248)
    c.sqs = c.at("sqs", [128, 1024], BF16, 17312)
    c.identf = c.at("identf", [128, 128], F32, 19360)
    c.onesf = c.at("onesf", [128, 128], F32, 19872)
    c.rows_c = c.at("rows_c", [1, 512], F32, 20384)
    c.kmaxbf = c.at("kmaxbf", [1, 32], BF16, 22400)
    c.kmaxsb = c.at("kmaxsb", [128, 16], F32, 22496)
    c.bcR_sb = c.at("bcR_sb", [128, 512], BF16, 22560)
    c.bcM_sb = c.at("bcM_sb", [128, 512], BF16, 23584)
    c.CBASE2 = None
    c.sel64 = c.at("sel64", [128, 2], BF16, 22432)
    c.sel32 = c.at("sel32", [128, 4], BF16, 22464)
    c.CBASE = 24608

    c.ps_a = nc.alloc_psum_tensor("ps_a", [128, 512], F32)
    c.ps_b = nc.alloc_psum_tensor("ps_b", [128, 512], F32)
    c.ps_c = nc.alloc_psum_tensor("ps_c", [128, 512], F32)
    c.ps_y = nc.alloc_psum_tensor("ps_y", [128, 1536], F32)
    c.pt_a = nc.alloc_psum_tensor("pt_a", [128, 512], BF16)
    c.pt_b = nc.alloc_psum_tensor("pt_b", [128, 512], BF16)
    for t in (c.ps_a, c.ps_b, c.ps_c, c.ps_y, c.pt_a, c.pt_b):
        c.trks[t.name] = Trk()

    kb.step("gpsimd", lambda e: e.memset(c.ident[:], 0.0), writes=[c.trk(c.ident)])
    kb.step("gpsimd", lambda e: e.affine_select(
        out=c.ident[:], in_=c.ident[:], compare_op=ALU.not_equal, fill=1.0,
        base=0, pattern=[[-1, 128]], channel_multiplier=1), writes=[c.trk(c.ident)])
    kb.step("gpsimd", lambda e: e.memset(c.ones_bf[:], 1.0), writes=[c.trk(c.ones_bf)])
    kb.step("gpsimd", lambda e: e.memset(c.identf[:], 0.0), writes=[c.trk(c.identf)])
    kb.step("gpsimd", lambda e: e.memset(c.onesf[:], 1.0), writes=[c.trk(c.onesf)])
    for (tt, nb, bw) in ((c.sel64, 2, 64), (c.sel32, 4, 32)):
        kb.step("gpsimd", lambda e, tt=tt: e.memset(tt[:], 1.0), writes=[c.trk(tt)])
        for cc in range(nb):
            kb.step("gpsimd", lambda e, tt=tt, cc=cc, bw=bw: e.affine_select(
                out=tt[:, cc:cc + 1], in_=tt[:, cc:cc + 1], compare_op=ALU.is_ge,
                fill=0.0, base=-bw * cc, pattern=[[0, 1]], channel_multiplier=1),
                reads=[c.trk(tt)], writes=[c.trk(tt)])
            kb.step("gpsimd", lambda e, tt=tt, cc=cc, bw=bw: e.affine_select(
                out=tt[:, cc:cc + 1], in_=tt[:, cc:cc + 1], compare_op=ALU.is_ge,
                fill=0.0, base=(bw * cc + bw - 1), pattern=[[0, 1]],
                channel_multiplier=-1),
                reads=[c.trk(tt)], writes=[c.trk(tt)])
    kb.step("gpsimd", lambda e: e.affine_select(
        out=c.identf[:], in_=c.identf[:], compare_op=ALU.not_equal, fill=1.0,
        base=0, pattern=[[-1, 128]], channel_multiplier=1), writes=[c.trk(c.identf)])
    kb.step("gpsimd", lambda e: e.memset(c.epsc[:], EPS), writes=[c.trk(c.epsc)])

    SC = c.CBASE
    t16 = c.at("rp_t16", [16, T], F32, SC)
    fr = c.at("rp_fr", [16, T], F32, SC + 8192)
    u = c.at("rp_u", [16, T], F32, SC + 16384)
    red = c.at("rp_red", [16, T], F32, SC + 24576)
    sn = c.at("rp_sn", [16, T], F32, SC + 32768)
    comb = c.at("rp_comb", [16, T], F32, SC + 40960)
    jcol = c.at("rp_jcol", [16, 1], F32, SC + 49152)
    inv = c.at("rp_inv", [16, 1], F32, SC + 49184)
    for nm in ("rp_t16", "rp_fr", "rp_u", "rp_red", "rp_sn", "rp_comb", "rp_jcol", "rp_inv"):
        pass
    kb.step("gpsimd", lambda e: e.iota(t16[:], pattern=[[1, T]], base=0,
            channel_multiplier=0, allow_small_or_imprecise_dtypes=True),
            writes=[c.trk(t16)])
    kb.step("gpsimd", lambda e: e.iota(jcol[:], pattern=[[0, 1]], base=0,
            channel_multiplier=2, allow_small_or_imprecise_dtypes=True),
            writes=[c.trk(jcol)])
    kb.step("scalar", lambda e: e.activation(inv[:], jcol[:], AF.Exp,
            scale=-math.log(10000.0) / 32.0), reads=[c.trk(jcol)], writes=[c.trk(inv)])
    kb.step("vector", lambda e: e.tensor_scalar(fr[:], t16[:], inv[:], None, ALU.mult),
            reads=[c.trk(t16), c.trk(inv)], writes=[c.trk(fr)])

    def reduce_to_pi(shift):
        if shift:
            kb.step("vector", lambda e: e.tensor_scalar(u[:], fr[:], 1.0 / TWO_PI,
                    shift / TWO_PI, ALU.mult, ALU.add),
                    reads=[c.trk(fr)], writes=[c.trk(u)])
        else:
            kb.step("vector", lambda e: e.tensor_scalar(u[:], fr[:], 1.0 / TWO_PI,
                    None, ALU.mult), reads=[c.trk(fr)], writes=[c.trk(u)])
        kb.step("vector", lambda e: e.tensor_scalar(u[:], u[:], MAGIC, MAGIC,
                ALU.add, ALU.subtract), reads=[c.trk(u)], writes=[c.trk(u)])
        kb.step("vector", lambda e: e.scalar_tensor_tensor(red[:], u[:], -HI2PI, fr[:],
                ALU.mult, ALU.add), reads=[c.trk(u), c.trk(fr)], writes=[c.trk(red)])
        if shift:
            kb.step("vector", lambda e: e.tensor_scalar(red[:], red[:],
                    float(np.float32(shift)), None, ALU.add),
                    reads=[c.trk(red)], writes=[c.trk(red)])
        kb.step("vector", lambda e: e.scalar_tensor_tensor(red[:], u[:], -LO2PI, red[:],
                ALU.mult, ALU.add), reads=[c.trk(u), c.trk(red)], writes=[c.trk(red)])
        kb.step("vector", lambda e: e.tensor_scalar(red[:], red[:], PI, None, ALU.min),
                reads=[c.trk(red)], writes=[c.trk(red)])
        kb.step("vector", lambda e: e.tensor_scalar(red[:], red[:], -PI, None, ALU.max),
                reads=[c.trk(red)], writes=[c.trk(red)])

    reduce_to_pi(0.0)
    kb.step("scalar", lambda e: e.activation(sn[:], red[:], AF.Sin),
            reads=[c.trk(red)], writes=[c.trk(sn)])
    reduce_to_pi(PI / 2)
    kb.step("scalar", lambda e: e.activation(comb[:], red[:], AF.Sin),
            reads=[c.trk(red)], writes=[c.trk(comb)])
    kb.step("vector", lambda e: e.tensor_add(comb[:], comb[:], sn[:]),
            reads=[c.trk(comb), c.trk(sn)], writes=[c.trk(comb)])
    rsem = kb.sem("rope_rep")
    for k in range(8):
        kb.step("gpsimd", lambda e, k=k: e.dma_start(
            c.rope4[16 * k:16 * k + 16, :], comb[:]),
            reads=[c.trk(comb)], writes=[c.trk(c.rope4)], sem=rsem, amt=16)
    core.kb.global_waits = kb.barrier_token() + [(rsem, rsem.n)]


def emit_ln(core, xbf, trk_x, ncols=QW, nch=CH, outs=None, trk_out=None,
            split=False):
    kb, c = core.kb, core
    pa, pb = c.ps_a, c.ps_b
    sq_par = [c.sqs[:, 0:512], c.sqs[:, 512:1024]]
    def s1(e):
        for ch in range(nch):
            i = e.matmul(pa[0:1, 0:ncols], c.ones_bf[:, 0:1],
                         xbf[:, ch * ncols:(ch + 1) * ncols],
                         start=(ch == 0), stop=(ch == nch - 1))
        return i
    kb.step("tensor", s1, reads=[trk_x, c.trk(c.ones_bf)], writes=[c.trk(pa)])
    for ch in range(nch):
        s = sq_par[ch % 2]
        xs = xbf[:, ch * ncols:(ch + 1) * ncols]
        kb.step("vector", lambda e, s=s, xs=xs: e.tensor_mul(
            s[:, 0:ncols], xs, xs), reads=[trk_x], writes=[c.trk(c.sqs)])
        kb.step("tensor", lambda e, s=s, ch=ch: e.matmul(
            pb[0:1, 0:ncols], c.ones_bf[:, 0:1], s[:, 0:ncols],
            start=(ch == 0), stop=(ch == nch - 1)),
            reads=[c.trk(c.sqs)], writes=[c.trk(pb)])
    n = float(nch * 128)
    kb.step("scalar", lambda e: e.activation(c.rows_a[0:1, 0:ncols], pa[0:1, 0:ncols],
            AF.Copy, scale=1.0 / n), reads=[c.trk(pa)], writes=[c.trk(c.rows_a)])
    kb.step("scalar", lambda e: e.activation(c.rows_c[0:1, 0:ncols], c.rows_a[0:1, 0:ncols],
            AF.Square), reads=[c.trk(c.rows_a)], writes=[c.trk(c.rows_c)])
    kb.step("scalar", lambda e: e.activation(c.rows_b[0:1, 0:ncols], pb[0:1, 0:ncols],
            AF.Copy, scale=1.0 / n), reads=[c.trk(pb)], writes=[c.trk(c.rows_b)])
    kb.step("vector", lambda e: e.tensor_sub(c.rows_b[0:1, 0:ncols],
            c.rows_b[0:1, 0:ncols], c.rows_c[0:1, 0:ncols]),
            reads=[c.trk(c.rows_b), c.trk(c.rows_c)], writes=[c.trk(c.rows_b)])
    kb.step("scalar", lambda e: e.activation(c.rows_b[0:1, 0:ncols], c.rows_b[0:1, 0:ncols],
            AF.Ln, bias=c.epsc[0:1, 0:1]), reads=[c.trk(c.rows_b), c.trk(c.epsc)],
            writes=[c.trk(c.rows_b)])
    kb.step("scalar", lambda e: e.activation(c.rows_b[0:1, 0:ncols], c.rows_b[0:1, 0:ncols],
            AF.Exp, scale=-0.5), reads=[c.trk(c.rows_b)], writes=[c.trk(c.rows_b)])
    kb.step("scalar", lambda e: e.activation(c.rows_rstd[0:1, 0:ncols], c.rows_b[0:1, 0:ncols],
            AF.Copy), reads=[c.trk(c.rows_b)], writes=[c.trk(c.rows_rstd)])
    kb.step("vector", lambda e: e.tensor_mul(c.rows_mr[0:1, 0:ncols], c.rows_a[0:1, 0:ncols],
            c.rows_b[0:1, 0:ncols]), reads=[c.trk(c.rows_a), c.trk(c.rows_b)],
            writes=[c.trk(c.rows_mr)])
    def finish():
        _emit_ln_finish(core, xbf, trk_x, ncols, nch, outs, trk_out)
    if split:
        return finish
    return finish()


def _emit_ln_finish(core, xbf, trk_x, ncols, nch, outs, trk_out):
    kb, c = core.kb, core
    bR, bM = c.ps_a, c.ps_b
    kb.step("tensor", lambda e: e.matmul(bR[:, 0:ncols], c.ones_bf[0:1, :],
            c.rows_rstd[0:1, 0:ncols], start=True, stop=True),
            reads=[c.trk(c.rows_rstd), c.trk(c.ones_bf)], writes=[c.trk(bR)])
    kb.step("tensor", lambda e: e.matmul(bM[:, 0:ncols], c.ones_bf[0:1, :],
            c.rows_mr[0:1, 0:ncols], start=True, stop=True),
            reads=[c.trk(c.rows_mr), c.trk(c.ones_bf)], writes=[c.trk(bM)])
    kb.step("scalar", lambda e: e.activation(c.bcR_sb[:, 0:ncols], bR[:, 0:ncols],
            AF.Copy), reads=[c.trk(bR)], writes=[c.trk(c.bcR_sb)])
    kb.step("scalar", lambda e: e.activation(c.bcM_sb[:, 0:ncols], bM[:, 0:ncols],
            AF.Copy), reads=[c.trk(bM)], writes=[c.trk(c.bcM_sb)])
    for ch in range(nch):
        dst = xbf[:, ch * ncols:(ch + 1) * ncols] if outs is None else outs[ch]
        to = trk_x if outs is None else trk_out
        kb.step("vector", lambda e, ch=ch, dst=dst: e.tensor_tensor(
            dst, xbf[:, ch * ncols:(ch + 1) * ncols],
            c.bcR_sb[:, 0:ncols], ALU.mult), reads=[trk_x, c.trk(c.bcR_sb)], writes=[to])
        kb.step("vector", lambda e, ch=ch, dst=dst: e.tensor_tensor(
            dst, dst, c.bcM_sb[:, 0:ncols], ALU.subtract),
            reads=[to, c.trk(c.bcM_sb)], writes=[to])


def emit_tail(core, ao, trk_ao, xf, trk_xf, src_x, trk_srcx, dst_x, trk_dstx,
              q, xfsem):
    kb, c = core.kb, core
    emit_ln(core, ao, trk_ao)
    for sub in range(2):
        cols = slice(q * QW + sub * 256, q * QW + sub * 256 + 256)
        srcv = src_x.rearrange("(c k) t -> k c t", k=128)[:, :, cols]
        dstv = dst_x.rearrange("(c k) t -> k c t", k=128)[:, :, cols]
        xfv = xf[:].rearrange("k (c t) -> k c t", c=CH)
        kb.step("sync", lambda e, d=xfv, s=srcv: e.dma_start(d, s),
                reads=[trk_srcx], writes=[trk_xf], sem=xfsem, amt=16)
        aov = ao[:].rearrange("k (c t) -> k c t", c=CH)[:, :, sub * 256:sub * 256 + 256]
        kb.step("vector", lambda e, xv=xf, av=aov: e.tensor_tensor(
            xv[:].rearrange("k (c t) -> k c t", c=CH), xv[:].rearrange("k (c t) -> k c t", c=CH),
            av, ALU.add), reads=[trk_xf, trk_ao], writes=[trk_xf],
            waits=[(xfsem, xfsem.n)])
        kb.step("sync", lambda e, d=dstv, s=xfv: e.dma_start(d, s),
                reads=[trk_xf], writes=[trk_dstx], sem=xfsem, amt=16)


class _TokTrk:
    def __init__(self, tok):
        self.tok = tok
    def rwait(self):
        return [self.tok]
    def wwait(self):
        return [self.tok]
    def setw(self, tok):
        pass
    def addr(self, tok):
        pass


def krtok_trk(tok):
    return _TokTrk(tok)


def merge_into(dst_trk, *srcs):
    for s in srcs:
        for sem, v in list(s.w.items()) + list(s.r.items()):
            dst_trk.r[sem] = max(dst_trk.r.get(sem, 0), v)


def emit_b0attn(core, seq, src_x, trk_srcx, dst_x, trk_dstx, st):
    kb, c = core.kb, core
    SB = c.CBASE
    S = seq
    kC = c.at(f"a0kC{S}", [128, CH * T], BF16, SB)
    Vtm = c.at(f"a0V{S}", [128, 16 * 1040], BF16, SB + 32768)
    kR = c.at(f"a0kR{S}", [128, T], BF16, SB + 66048)
    ckv = c.at(f"a0ckv{S}", [128, 2 * QW], BF16, SB + 70144)
    wdkv = c.at(f"a0wdkv{S}", [128, 2048], BF16, SB + 72192)
    wuk = c.at(f"a0wuk{S}", [128, 2048], BF16, SB + 76288)
    wuv = c.at(f"a0wuv{S}", [128, 2048], BF16, SB + 80384)
    wkr = c.at(f"a0wkr{S}", [128, 256], BF16, SB + 84480)
    skr = c.at(f"a0skr{S}", [32, QW], BF16, SB + 84992)
    sqR = c.at(f"a0sqR{S}", [128, QW], BF16, SB + 86016)
    sqRs = c.at(f"a0sqRs{S}", [32, QW], BF16, SB + 87040)
    xbf = [c.at(f"a0xbf{S}_{i}", [128, CH * QW], BF16, SB + 88064 + i * 8192)
           for i in range(2)]
    CQ0 = SB + 104448
    cQ = c.at(f"a0cQ{S}", [128, 32 * QW], BF16, CQ0)
    yfm = c.at(f"a0yfm{S}", [128, CH * QW], BF16, CQ0)
    ao = c.at(f"a0ao{S}", [128, CH * QW], BF16, CQ0 + 8192)
    xf = c.at(f"a0xf{S}", [128, CH * 256], F32, CQ0 + 16384)
    yn = c.at(f"a0yn{S}", [128, 1024], BF16, CQ0 + 24576)
    qC = c.at(f"a0qC{S}", [128, CH * QW], BF16, SB + 137216)
    qR = c.at(f"a0qR{S}", [128, 4 * QW], BF16, SB + 145408)
    qRs = c.at(f"a0qRs{S}", [32, 4 * QW], BF16, SB + 149504)
    probs = [c.at(f"a0pr{S}_{i}", [128, QW], BF16, SB + 153600 + i * 1024)
             for i in range(3)]
    probsT = [c.at(f"a0pT{S}_{i}", [128, QW], BF16, SB + 156672 + i * 1024)
              for i in range(3)]
    vm = [c.at(f"a0vm{S}_{i}", [128, QW], BF16, SB + 159744 + i * 1024)
          for i in range(2)]

    xlsem = [kb.sem(f"a0xl{S}_{i}") for i in range(2)]
    xfsem = kb.sem(f"a0xf{S}")
    kwsem = kb.sem(f"a0kw{S}")
    repsem = kb.sem(f"a0rep{S}")

    for (dst, key) in ((wdkv, "wdkv"), (wuk, "wuk"), (wuv, "wuv"), (wkr, "wkr")):
        src = c.Wsrc[key]
        nb, P, F = src.shape
        kb.step("gpsimd", lambda e, d=dst, s=src, nb=nb, F=F: e.dma_start(
            d[:, 0:nb * F].rearrange("p (g f) -> p g f", g=nb),
            s.rearrange("g p f -> p g f")),
            writes=[c.trk(dst)], sem=kwsem, amt=16)
    kwtok = (kwsem, kwsem.n)
    c.dump("wuk", wuk[:], [c.trk(wuk), krtok_trk(kwtok)])
    c.dump("rope4", c.rope4[:], [c.trk(c.rope4)])
    c.dump("wdkv", wdkv[:], [c.trk(wdkv), krtok_trk(kwtok)])
    c.dump("wkr", wkr[:], [c.trk(wkr), krtok_trk(kwtok)])

    rot = [c.ps_a, c.ps_b, c.ps_c]

    def mm_group(pbank, prow_sl, pcol_sl, tiles, rhss, reads, extra_waits=()):
        def f(e, tiles=tuple(tiles), rhss=tuple(rhss)):
            n = len(tiles)
            for i in range(n):
                ins = e.matmul(pbank[prow_sl, pcol_sl], tiles[i], rhss[i],
                               start=(i == 0), stop=(i == n - 1))
            return ins
        return kb.step("tensor", f, reads=list(reads), writes=[c.trk(pbank)],
                       waits=list(extra_waits))

    sqp = [c.sqs[:, 0:512], c.sqs[:, 512:1024]]

    def emit_A0(q, split=False):
        xb = xbf[q % 2]
        txb = c.trk(xb)
        srcv = src_x.rearrange("(c k) t -> k c t", k=128)[:, :, q * QW:(q + 1) * QW]
        kb.step("gpsimd", lambda e, d=xb, s=srcv: e.dma_start(
            d[:].rearrange("k (c t) -> k c t", c=CH), s),
            reads=[trk_srcx], writes=[txb], sem=xlsem[q % 2], amt=16)
        return emit_ln(core, xb, txb, split=split)

    emit_A0(0)
    for q in range(NQ):
        qcols = slice(q * QW, (q + 1) * QW)
        xb = xbf[q % 2]
        txb = c.trk(xb)
        for m in range(2):
            pb = rot[m % 3]
            mm_group(pb, slice(None), slice(0, QW),
                     [wdkv[:, m * 1024 + cc * 128: m * 1024 + (cc + 1) * 128] for cc in range(CH)],
                     [xb[:, cc * QW:(cc + 1) * QW] for cc in range(CH)],
                     [txb, c.trk(wdkv)], extra_waits=[kwtok])
            kb.step("vector", lambda e, m=m, pb=pb: e.tensor_copy(
                ckv[:, m * QW:(m + 1) * QW], pb[:, 0:QW]),
                reads=[c.trk(pb)], writes=[c.trk(ckv)])
        for m in range(CH):
            pb = rot[m % 3]
            mm_group(pb, slice(None), slice(0, QW),
                     [wuk[:, m * 256 + cc * 128: m * 256 + (cc + 1) * 128] for cc in range(2)],
                     [ckv[:, cc * QW:(cc + 1) * QW] for cc in range(2)],
                     [c.trk(ckv), c.trk(wuk)])
            dst_kc = kC[:, m * T + q * QW: m * T + (q + 1) * QW]
            kb.step("vector", lambda e, d=dst_kc, pb=pb: e.tensor_copy(
                d, pb[:, 0:QW]), reads=[c.trk(pb)], writes=[c.trk(kC)])
        for m in range(CH):
            pb = rot[m % 3]
            vmb = vm[m % 2]
            mm_group(pb, slice(None), slice(0, QW),
                     [wuv[:, m * 256 + cc * 128: m * 256 + (cc + 1) * 128] for cc in range(2)],
                     [ckv[:, cc * QW:(cc + 1) * QW] for cc in range(2)],
                     [c.trk(ckv), c.trk(wuv)])
            kb.step("vector", lambda e, pb=pb, vmb=vmb: e.tensor_copy(
                vmb[:], pb[:, 0:QW]), reads=[c.trk(pb)], writes=[c.trk(vmb)])
            for tw in range(4):
                kc = 4 * q + tw
                ptb = [c.pt_a, c.pt_b][tw % 2]
                kb.step("tensor", lambda e, vmb=vmb, tw=tw, ptb=ptb: e.transpose(
                    ptb[:, 0:128], vmb[:, tw * 128:(tw + 1) * 128], c.ident[:]),
                    reads=[c.trk(vmb), c.trk(c.ident)], writes=[c.trk(ptb)])
                dstv = Vtm[:, kc * 1040 + 65 * (2 * m): kc * 1040 + 65 * (2 * m) + 130]
                kb.step("vector", lambda e, dstv=dstv, ptb=ptb: e.tensor_copy(
                    dstv.rearrange("p (h d) -> p h d", h=2)[:, :, 0:64],
                    ptb[:, 0:128].rearrange("p (h d) -> p h d", h=2)),
                    reads=[c.trk(ptb)], writes=[c.trk(Vtm)])
        for tw in range(4):
            kc = 4 * q + tw
            ones_ap = Vtm[:, kc * 1040: (kc + 1) * 1040].rearrange(
                "p (h d) -> p h d", d=65)[:, :, 64:65]
            kb.step("gpsimd", lambda e, a=ones_ap: e.memset(a, 1.0), writes=[c.trk(Vtm)])
        pb = rot[0]
        mm_group(pb, slice(0, 32), slice(0, QW),
                 [wkr[:, cc * 32:(cc + 1) * 32] for cc in range(CH)],
                 [xb[:, cc * QW:(cc + 1) * QW] for cc in range(CH)],
                 [txb, c.trk(wkr)], extra_waits=[kwtok])
        kr_dst = kR[0:32, qcols]
        kr_rope = c.rope4[0:32, qcols]
        kb.step("vector", lambda e, pb=pb, d=kr_dst, rp=kr_rope: e.tensor_tensor(
            d, pb[0:32, 0:QW], rp, ALU.mult),
            reads=[c.trk(pb), c.trk(c.rope4)], writes=[c.trk(kR)])
        for j in range(1, 4):
            kr_to = kR[32 * j:32 * (j + 1), qcols]
            kb.step("gpsimd", lambda e, d=kr_to, s=kr_dst: e.dma_start(d, s),
                reads=[c.trk(kR)], writes=[c.trk(kR)], sem=repsem, amt=16)
        krtok = (repsem, repsem.n)
        c.dump("ckv0", ckv[:], [c.trk(ckv)])
        c.dump("kC0", kC[:, 0:T], [c.trk(kC)])
        c.dump("kR0", kR[:, 0:QW], [c.trk(kR), krtok_trk(krtok)])
        c.dump("vtm0", Vtm[:, 0:2080], [c.trk(Vtm)])
        kb.step("scalar", lambda e, s=kr_dst: e.activation(skr[0:32, 0:QW], s,
                AF.Square), reads=[c.trk(kR)], waits=[krtok], writes=[c.trk(skr)])
        for m in range(CH):
            sq = sqp[m % 2]
            tsq = c.trk(c.sqs)
            src_kc = kC[:, m * T + q * QW: m * T + (q + 1) * QW]
            kb.step("scalar", lambda e, sq=sq, s=src_kc: e.activation(
                sq, s, AF.Square), reads=[c.trk(kC)], writes=[tsq])
            for hh in range(2):
                h = 2 * m + hh
                prow = rot[h % 3]
                def knorm(e, sq=sq, hh=hh, prow=prow):
                    e.matmul(prow[0:1, 0:QW], c.ones_bf[64 * hh:64 * hh + 64, 0:1],
                             sq[64 * hh:64 * hh + 64, :], start=True, stop=False)
                    return e.matmul(prow[0:1, 0:QW], c.ones_bf[0:32, 0:1],
                                    skr[0:32, 0:QW], start=False, stop=True)
                kb.step("tensor", knorm, reads=[tsq, c.trk(skr), c.trk(c.ones_bf)],
                        writes=[c.trk(prow)])
                if q == 0:
                    kb.step("vector", lambda e, h=h, prow=prow: e.reduce_max(
                        c.kmax[0:1, h:h + 1], prow[0:1, 0:QW], axis=AX.X),
                        reads=[c.trk(prow)], writes=[c.trk(c.kmax)])
                else:
                    kb.step("vector", lambda e, h=h, prow=prow: e.reduce_max(
                        c.kmax[0:1, 16 + h:17 + h], prow[0:1, 0:QW], axis=AX.X),
                        reads=[c.trk(prow)], writes=[c.trk(c.kmax)])
                    kb.step("vector", lambda e, h=h: e.tensor_max(
                        c.kmax[0:1, h:h + 1], c.kmax[0:1, h:h + 1],
                        c.kmax[0:1, 16 + h:17 + h]),
                        reads=[c.trk(c.kmax)], writes=[c.trk(c.kmax)])
        c.dump("kmax", c.kmax[:], [c.trk(c.kmax)])
        for bi in range(16):
            hld = st["wdq"].load(bi)
            for g in range(2):
                m = bi * 2 + g
                pb = rot[m % 3]
                mm_group(pb, slice(None), slice(0, QW),
                         [hld[0][:, g * 1024 + cc * 128: g * 1024 + (cc + 1) * 128]
                          for cc in range(CH)],
                         [xb[:, cc * QW:(cc + 1) * QW] for cc in range(CH)],
                         [txb, hld[1]], extra_waits=[hld[2]])
                kb.step("vector", lambda e, m=m, pb=pb: e.tensor_copy(
                    cQ[:, m * QW:(m + 1) * QW], pb[:, 0:QW]),
                    reads=[c.trk(pb)], writes=[c.trk(cQ)])
        for m in range(CH):
            hld = st["wuq"].load(m)
            pb = rot[m % 3]
            mm_group(pb, slice(None), slice(0, QW),
                     [hld[0][:, cc * 128:(cc + 1) * 128] for cc in range(32)],
                     [cQ[:, cc * QW:(cc + 1) * QW] for cc in range(32)],
                     [c.trk(cQ), hld[1]], extra_waits=[hld[2]])
            kb.step("vector", lambda e, m=m, pb=pb: e.tensor_copy(
                qC[:, m * QW:(m + 1) * QW], pb[:, 0:QW]),
                reads=[c.trk(pb)], writes=[c.trk(qC)])
        for m in range(4):
            hld = st["wqr"].load(m)
            pb = rot[m % 3]
            mm_group(pb, slice(None), slice(0, QW),
                     [hld[0][:, cc * 128:(cc + 1) * 128] for cc in range(32)],
                     [cQ[:, cc * QW:(cc + 1) * QW] for cc in range(32)],
                     [c.trk(cQ), hld[1]], extra_waits=[hld[2]])
            qr_rope = c.rope4[:, qcols]
            kb.step("vector", lambda e, m=m, pb=pb, rp=qr_rope: e.tensor_tensor(
                qR[:, m * QW:(m + 1) * QW], pb[:, 0:QW], rp, ALU.mult),
                reads=[c.trk(pb), c.trk(c.rope4)], writes=[c.trk(qR)])
            kb.step("gpsimd", lambda e, m=m: e.dma_start(
                qRs[0:32, m * QW:(m + 1) * QW], qR[96:128, m * QW:(m + 1) * QW]),
                reads=[c.trk(qR)], writes=[c.trk(qRs)], sem=repsem, amt=16)
        qstok = (repsem, repsem.n)
        c.dump("cQ0", cQ[:, 0:QW], [c.trk(cQ)])
        c.dump("qC0", qC[:], [c.trk(qC)])
        c.dump("qR0", qR[:], [c.trk(qR)])
        for ov in (yfm, ao, xf, yn):
            merge_into(c.trk(ov), c.trk(cQ))
        kb.step("scalar", lambda e: e.activation(c.kmaxbf[0:1, 0:16],
                c.kmax[0:1, 0:16], AF.Copy), reads=[c.trk(c.kmax)],
                writes=[c.trk(c.kmaxbf)])
        kb.step("tensor", lambda e: e.matmul(c.ps_c[:, 16:32], c.ones_bf[0:1, :],
                c.kmaxbf[0:1, 0:16], start=True, stop=True),
                reads=[c.trk(c.kmaxbf), c.trk(c.ones_bf)], writes=[c.trk(c.ps_c)])
        fl = c.pe_flush()
        kb.step("vector", lambda e: e.tensor_copy(c.kmaxsb[:], c.ps_c[:, 16:32]),
                reads=[c.trk(c.ps_c)], writes=[c.trk(c.kmaxsb)], waits=[fl])
        c.dump("kmaxsb", c.kmaxsb[:], [c.trk(c.kmaxsb)])
        pcnt = [0]
        for qq in range(4):
            i_qt = 4 * q + qq
            klen = (i_qt + 1) * 128
            first_nm = [True]
            for mc in range(CH):
                sq = sqp[mc % 2]
                srcs = qC[:, mc * QW + qq * 128: mc * QW + (qq + 1) * 128]
                kb.step("gpsimd", lambda e, sq=sq, srcs=srcs: e.tensor_mul(
                    sq[:, 0:128], srcs, srcs), reads=[c.trk(qC)], writes=[c.trk(c.sqs)])
                kb.step("tensor", lambda e, mc=mc, sq=sq, f=first_nm[0]: e.matmul(
                    c.ps_c[:, 2 * mc:2 * mc + 2], sq[:, 0:128], c.sel64[:],
                    start=f, stop=False, skip_group_check=True),
                    reads=[c.trk(c.sqs), c.trk(c.sel64)], writes=[c.trk(c.ps_c)])
                first_nm[0] = False
            for mq in range(4):
                sq = sqp[mq % 2]
                srcs = qR[:, mq * QW + qq * 128: mq * QW + (qq + 1) * 128]
                kb.step("gpsimd", lambda e, sq=sq, srcs=srcs: e.tensor_mul(
                    sq[:, 0:128], srcs, srcs), reads=[c.trk(qR)], writes=[c.trk(c.sqs)])
                kb.step("tensor", lambda e, mq=mq, sq=sq, last=(mq == 3): e.matmul(
                    c.ps_c[:, 4 * mq:4 * mq + 4], sq[:, 0:128], c.sel32[:],
                    start=False, stop=last, skip_group_check=True),
                    reads=[c.trk(c.sqs), c.trk(c.sel32)], writes=[c.trk(c.ps_c)])
            fl2 = c.pe_flush()
            kb.step("vector", lambda e: e.tensor_copy(c.bias3[:, 0:16], c.ps_c[:, 0:16]),
                    reads=[c.trk(c.ps_c)], writes=[c.trk(c.bias3)], waits=[fl2])
            kb.step("vector", lambda e: e.tensor_tensor(c.bias3[:, 16:32],
                    c.bias3[:, 0:16], c.kmaxsb[:], ALU.mult),
                    reads=[c.trk(c.bias3), c.trk(c.kmaxsb)], writes=[c.trk(c.bias3)])
            kb.step("scalar", lambda e: e.activation(c.bias3[:, 32:48],
                    c.bias3[:, 16:32], AF.Sqrt), reads=[c.trk(c.bias3)],
                    writes=[c.trk(c.bias3)])
            kb.step("vector", lambda e: e.tensor_scalar(c.bias3[:, 32:48],
                    c.bias3[:, 32:48], -SCALE0, None, ALU.mult),
                    reads=[c.trk(c.bias3)], writes=[c.trk(c.bias3)])
            c.dump("bias0", c.bias3[:], [c.trk(c.bias3)])
            nw = ceil_div(klen, QW)
            units = [(h, w) for h in range(H0) for w in range(nw)]

            def emit_sc(h, w):
                mc, hc, j, mq = h // 2, h % 2, h % 4, h // 4
                L = min(QW, klen - w * QW)
                par = pcnt[0] % 3
                pcnt[0] += 1
                sc = [c.ps_a, c.ps_b, c.ps_c][par]
                pr = probs[par]
                qC_l = qC[:, mc * QW + qq * 128: mc * QW + (qq + 1) * 128][
                    64 * hc:64 * hc + 64, :]
                if j < 3:
                    qR_l = qR[:, mq * QW + qq * 128: mq * QW + (qq + 1) * 128][
                        32 * j:32 * j + 32, :]
                    kR_l = kR[32 * j:32 * j + 32, w * QW:w * QW + L]
                else:
                    qR_l = qRs[0:32, mq * QW + qq * 128: mq * QW + (qq + 1) * 128]
                    kR_l = kR[0:32, w * QW:w * QW + L]
                kC_l = kC[:, mc * T + w * QW: mc * T + w * QW + L][
                    64 * hc:64 * hc + 64, :]
                def scmm(e, sc=sc, qC_l=qC_l, kC_l=kC_l, qR_l=qR_l, kR_l=kR_l, L=L):
                    e.matmul(sc[:, 0:L], qC_l, kC_l, start=True, stop=False)
                    ins = e.matmul(sc[:, 0:L], qR_l, kR_l, start=False, stop=True)
                    if L < 384:
                        ins = e.matmul(sc[0:1, 0:2], c.ident[0:1, 1:2],
                                       c.ones_bf[0:1, 0:2], start=False, stop=True,
                                       skip_group_check=True)
                    return ins
                kb.step("tensor", scmm,
                        reads=[c.trk(qC), c.trk(qR), c.trk(qRs), c.trk(kC), c.trk(kR)],
                        writes=[c.trk(sc)])
                kb.step("scalar", lambda e, sc=sc, pr=pr, h=h, L=L: e.activation(
                    pr[:, 0:L], sc[:, 0:L], AF.Exp,
                    bias=c.bias3[:, 32 + h:33 + h], scale=SCALE0),
                    reads=[c.trk(sc), c.trk(c.bias3)], writes=[c.trk(pr)])
                if w == nw - 1:
                    kb.step("gpsimd", lambda e, pr=pr, L=L: e.affine_select(
                        out=pr[:, L - 128:L], in_=pr[:, L - 128:L],
                        compare_op=ALU.is_ge, fill=0.0, base=0,
                        pattern=[[-1, 128]], channel_multiplier=1),
                        reads=[c.trk(pr)], writes=[c.trk(pr)])
                return par

            def emit_tr(h, w, par):
                L = min(QW, klen - w * QW)
                pr = probs[par]
                pT = probsT[par]
                ptp = [c.pt_a, c.pt_b][par % 2]
                nch = ceil_div(L, 128)
                def trgrp(e, pr=pr, ptp=ptp, nch=nch):
                    for tch in range(nch):
                        ins = e.transpose(ptp[:, tch * 128:(tch + 1) * 128],
                                          pr[:, tch * 128:(tch + 1) * 128], c.ident[:])
                    return ins
                kb.step("tensor", trgrp, reads=[c.trk(pr), c.trk(c.ident)],
                        writes=[c.trk(ptp)])
                kb.step("vector", lambda e, pT=pT, ptp=ptp, L=L: e.tensor_copy(
                    pT[:, 0:L], ptp[:, 0:L]), reads=[c.trk(ptp)], writes=[c.trk(pT)])

            def emit_pv(h, w, par):
                L = min(QW, klen - w * QW)
                pT = probsT[par]
                nch = ceil_div(L, 128)
                yc = ycol0(h)
                def pvgrp(e, pT=pT, h=h, w=w, nch=nch, yc=yc,
                          first=(w == 0), last=(w == nw - 1)):
                    for tch in range(nch):
                        kc = 4 * w + tch
                        ins = e.matmul(c.ps_y[:, yc:yc + 65],
                                       pT[:, tch * 128:(tch + 1) * 128],
                                       Vtm[:, kc * 1040 + 65 * h: kc * 1040 + 65 * (h + 1)],
                                       start=(first and tch == 0),
                                       stop=(last and tch == nch - 1))
                    return ins
                kb.step("tensor", pvgrp, reads=[c.trk(pT), c.trk(Vtm)],
                        writes=[c.trk(c.ps_y)])

            plist = []
            for u in units:
                par = emit_sc(*u)
                plist.append((u[0], u[1], par))
                if len(plist) >= 2:
                    emit_tr(*plist[-2])
                if len(plist) >= 3:
                    emit_pv(*plist[-3])
            emit_tr(*plist[-1])
            if len(plist) >= 2:
                emit_pv(*plist[-2])
            emit_pv(*plist[-1])
            fl3 = c.pe_flush()
            for g, ng in ((0, 7), (1, 7), (2, 2)):
                kb.step("vector", lambda e, g=g, ng=ng: e.reciprocal(
                    c.recips[:, 7 * g:7 * g + ng],
                    bass.AP(c.ps_y, 512 * g + 64, [[1536, 128], [65, ng]])),
                    reads=[c.trk(c.ps_y)], writes=[c.trk(c.recips)], waits=[fl3])
            for h in range(H0):
                yc = ycol0(h)
                kb.step("scalar", lambda e, h=h, yc=yc: e.mul(
                    yn[:, 64 * h:64 * (h + 1)], c.ps_y[:, yc:yc + 64],
                    c.recips[:, h:h + 1]),
                    reads=[c.trk(c.ps_y), c.trk(c.recips)], writes=[c.trk(yn)])
            c.dump("pr0", probs[0][:], [c.trk(probs[0])])
            c.dump("yn0", yn[:], [c.trk(yn)])
            c.dump("recips0", c.recips[:], [c.trk(c.recips)])
            for half in range(2):
                ptp = [c.pt_a, c.pt_b][half]
                def ytr(e, half=half, ptp=ptp):
                    for c2 in range(4):
                        ins = e.transpose(ptp[:, c2 * 128:(c2 + 1) * 128],
                                          yn[:, (half * 4 + c2) * 128:(half * 4 + c2 + 1) * 128],
                                          c.ident[:])
                    return ins
                kb.step("tensor", ytr, reads=[c.trk(yn), c.trk(c.ident)], writes=[c.trk(ptp)])
                dstv = yfm[:].rearrange("p (c t) -> p c t", c=CH)[
                    :, half * 4:(half + 1) * 4, qq * 128:(qq + 1) * 128]
                kb.step("vector", lambda e, ptp=ptp, dstv=dstv: e.tensor_copy(
                    dstv, ptp[:].rearrange("p (c t) -> p c t", c=4)),
                    reads=[c.trk(ptp)], writes=[c.trk(yfm)])
        c.dump("yfm0", yfm[:], [c.trk(yfm)])
        fin0 = emit_A0(q + 1, split=True) if q + 1 < NQ else None
        for bi in range(2):
            hld = st["wo0"].load(bi)
            for g in range(4):
                m = bi * 4 + g
                pb = rot[m % 3]
                mm_group(pb, slice(None), slice(0, QW),
                         [hld[0][:, g * 1024 + cc * 128: g * 1024 + (cc + 1) * 128]
                          for cc in range(CH)],
                         [yfm[:, cc * QW:(cc + 1) * QW] for cc in range(CH)],
                         [c.trk(yfm), hld[1]], extra_waits=[hld[2]])
                kb.step("vector", lambda e, m=m, pb=pb: e.tensor_copy(
                    ao[:, m * QW:(m + 1) * QW], pb[:, 0:QW]),
                    reads=[c.trk(pb)], writes=[c.trk(ao)])
        if fin0 is not None:
            fin0()
        c.dump("ao0", ao[:], [c.trk(ao)])
        emit_tail(core, ao, c.trk(ao), xf, c.trk(xf), src_x, trk_srcx,
                  dst_x, trk_dstx, q, xfsem)
        merge_into(c.trk(cQ), c.trk(yfm), c.trk(ao), c.trk(xf), c.trk(yn))


WSPECS = [
    ("wdq", (32, 128, 1024), F32), ("wuq", (8, 128, 4096), F32),
    ("wqr", (4, 128, 4096), F32), ("wdkv", (2, 128, 1024), F32),
    ("wuk", (8, 128, 256), F32), ("wuv", (8, 128, 256), F32),
    ("wkr", (1, 128, 256), F32), ("wo0", (8, 128, 1024), F32),
    ("wg", (24, 128, 1024), I32), ("wu", (24, 128, 1024), I32),
    ("wd", (8, 128, 3072), I32),
    ("wq1", (8, 128, 1024), F32), ("wkd", (1, 128, 512), F32),
    ("wvd", (1, 128, 512), F32), ("wku", (1, 32, 128), F32),
    ("wvu", (1, 32, 128), F32), ("wo1", (8, 128, 1024), F32),
    ("kern", (8, 128, 1024), F32), ("bias1", (128, 8), F32),
    ("la", (1, 128, 128), F32), ("lb", (8, 16, 128), F32),
]


def build_program(stages=("b0attn", "b0ffn", "b1attn", "b1ffn"), dbg=False):
    nc = bass.Bass("TRN2", target_bir_lowering=False)
    kb = Kb(nc)
    core = Core(nc, kb)
    core.kb = kb
    core.dbg = dbg
    kb.global_waits = []
    xin = nc.dram_tensor("x", [NSEQ, 1024, T], F32, kind="ExternalInput").ap()
    out = nc.dram_tensor("out", [NSEQ, 1024, T], F32, kind="ExternalOutput").ap()
    xs0 = nc.dram_tensor("xs0", [1024, T], F32, kind="Internal").ap()
    xs1 = nc.dram_tensor("xs1", [1024, T], F32, kind="Internal").ap()
    core.Wsrc = {}
    for name, shape, dt in WSPECS:
        core.Wsrc[name] = nc.dram_tensor(name, list(shape), dt, kind="ExternalInput").ap()
    t_xin = Trk(); t_out = Trk(); t_xs0 = Trk(); t_xs1 = Trk()

    setup_consts(core)

    WB0 = core.CBASE + 161792
    st = {}
    st["wdq"] = WStream(kb, "wdq", core.Wsrc["wdq"], 32, 128, 1024,
                        core.MBASE + WB0, batch=2)
    st["wuq"] = WStream(kb, "wuq", core.Wsrc["wuq"], 8, 128, 4096,
                        core.MBASE + WB0 + 8192, batch=1)
    st["wqr"] = WStream(kb, "wqr", core.Wsrc["wqr"], 4, 128, 4096,
                        None, batch=1, share=st["wuq"])
    st["wo0"] = WStream(kb, "wo0", core.Wsrc["wo0"], 8, 128, 1024,
                        None, batch=4, share=st["wuq"])
    FWB = core.MBASE + core.CBASE + 147456
    st["wg"] = WStream(kb, "wg", core.Wsrc["wg"], 24, 128, 1024, FWB, batch=2)
    st["wu"] = WStream(kb, "wu", core.Wsrc["wu"], 24, 128, 1024, FWB + 8192, batch=2)
    st["wd"] = WStream(kb, "wd", core.Wsrc["wd"], 8, 128, 3072, FWB + 16384, batch=1)

    emitters = {"b0attn": emit_b0attn, "b0ffn": emit_b0ffn,
                "b1attn": emit_b1attn, "b1ffn": emit_b1ffn}
    all_stages = ["b0attn", "b0ffn", "b1attn", "b1ffn"]
    active = [s for s in all_stages if s in stages]
    for seq in range(NSEQ):
        src, tsrc = xin[seq], t_xin
        scratch = [(xs0, t_xs0), (xs1, t_xs1)]
        for k, sname in enumerate(active):
            if k == len(active) - 1:
                dst, tdst = out[seq], t_out
            else:
                dst, tdst = scratch[k % 2]
            kb.global_waits = kb.barrier_token()
            emitters[sname](core, seq, src, tsrc, dst, tdst, st)
            src, tsrc = dst, tdst
    dummy = core.at("fin_dummy", [1, 4], F32, core.CBASE + 184320 - 64)
    kb.step("gpsimd", lambda e: e.memset(dummy[:], 0.0), reads=[t_out],
            writes=[core.trk(dummy)], waits=list(getattr(kb, "dbg_sems", [])))
    with nc.Block() as block:
        kb.finalize(block)
    return nc


def kernel(**inputs):
    Wp, x_fm = prep_host(inputs)
    nc = build_program()
    in_maps = []
    for cid in range(8):
        m = {"x": np.ascontiguousarray(x_fm[2 * cid:2 * cid + 2])}
        m.update(Wp)
        in_maps.append(m)
    res = run_bass_kernel_spmd(nc, in_maps, core_ids=list(range(8)))
    y = np.concatenate([r["out"] for r in res.results], 0)
    return np.ascontiguousarray(y.transpose(0, 2, 1))


def emit_b0ffn(core, seq, src_x, trk_srcx, dst_x, trk_dstx, st):
    kb, c = core.kb, core
    SB = c.CBASE
    S = seq
    xn = c.at(f"f0xn{S}", [128, CH * T], BF16, SB)
    hf = c.at(f"f0h{S}", [128, FFC * T], BF16, SB + 32768)
    xbf = [c.at(f"f0xbf{S}_{i}", [128, CH * QW], BF16, SB + 131072 + i * 8192)
           for i in range(2)]
    out_q = c.at(f"f0out{S}", [128, CH * QW], BF16, SB + 131072)
    xf = c.at(f"f0xf{S}", [128, CH * 256], F32, SB + 139264)
    WB = SB + 147456
    xlsem = [kb.sem(f"f0xl{S}_{i}") for i in range(2)]
    xfsem = kb.sem(f"f0xf{S}")

    sqp = [c.sqs[:, 0:512], c.sqs[:, 512:1024]]
    rot4 = [c.ps_a, c.ps_b, c.ps_c, None]

    for q in range(NQ):
        xb = xbf[q % 2]
        txb = c.trk(xb)
        srcv = src_x.rearrange("(c k) t -> k c t", k=128)[:, :, q * QW:(q + 1) * QW]
        kb.step("gpsimd", lambda e, d=xb, s=srcv: e.dma_start(
            d[:].rearrange("k (c t) -> k c t", c=CH), s),
            reads=[trk_srcx], writes=[txb], sem=xlsem[q % 2], amt=16)
        outs = [xn[:, ch * T + q * QW: ch * T + (q + 1) * QW] for ch in range(CH)]
        emit_ln(core, xb, txb, outs=outs, trk_out=c.trk(xn))
    merge_into(c.trk(out_q), c.trk(xbf[0]))
    merge_into(c.trk(xf), c.trk(xbf[1]))
    c.dump("f_xn", xn[:, 0:T], [c.trk(xn)])

    for bi in range(12):
        hg = st["wg"].load(bi)
        hu = st["wu"].load(bi)
        for g in range(2):
            m = bi * 2 + g
            for w in range(NQ):
                pg = rot4[(2 * w) % 3]
                pu = rot4[(2 * w + 1) % 3]
                sg = sqp[w % 2]
                def gmm(e, hg=hg, g=g, w=w, pg=pg):
                    for cc in range(CH):
                        ins = e.matmul(pg[:, 0:QW],
                                       hg[0][:, g * 1024 + cc * 128:g * 1024 + (cc + 1) * 128],
                                       xn[:, cc * T + w * QW: cc * T + (w + 1) * QW],
                                       start=(cc == 0), stop=(cc == CH - 1))
                    return ins
                kb.step("tensor", gmm, reads=[c.trk(xn), hg[1]], writes=[c.trk(pg)],
                        waits=[hg[2]])
                def umm(e, hu=hu, g=g, w=w, pu=pu):
                    for cc in range(CH):
                        ins = e.matmul(pu[:, 0:QW],
                                       hu[0][:, g * 1024 + cc * 128:g * 1024 + (cc + 1) * 128],
                                       xn[:, cc * T + w * QW: cc * T + (w + 1) * QW],
                                       start=(cc == 0), stop=(cc == CH - 1))
                    return ins
                kb.step("tensor", umm, reads=[c.trk(xn), hu[1]], writes=[c.trk(pu)],
                        waits=[hu[2]])
                kb.step("scalar", lambda e, pg=pg, sg=sg: e.activation(
                    sg, pg[:, 0:QW], AF.Silu, scale=1.0 / 127.0),
                    reads=[c.trk(pg)], writes=[c.trk(c.sqs)])
                hdst = hf[:, m * T + w * QW: m * T + (w + 1) * QW]
                kb.step("vector", lambda e, hdst=hdst, sg=sg, pu=pu: e.tensor_tensor(
                    hdst, sg, pu[:, 0:QW], ALU.mult),
                    reads=[c.trk(c.sqs), c.trk(pu)], writes=[c.trk(hf)])

    c.dump("f_h", hf[:, 0:T], [c.trk(hf)])
    for q in range(NQ):
        for m2 in range(CH):
            hw = st["wd"].load(m2)
            pb = rot4[m2 % 3]
            def dmm(e, hw=hw, pb=pb, q=q):
                for kc in range(FFC):
                    ins = e.matmul(pb[:, 0:QW], hw[0][:, kc * 128:(kc + 1) * 128],
                                   hf[:, kc * T + q * QW: kc * T + (q + 1) * QW],
                                   start=(kc == 0), stop=(kc == FFC - 1))
                return ins
            kb.step("tensor", dmm, reads=[c.trk(hf), hw[1]], writes=[c.trk(pb)],
                    waits=[hw[2]])
            kb.step("scalar", lambda e, m2=m2, pb=pb: e.activation(
                out_q[:, m2 * QW:(m2 + 1) * QW], pb[:, 0:QW], AF.Copy,
                scale=1.0 / (127.0 * 127.0)),
                reads=[c.trk(pb)], writes=[c.trk(out_q)])
        c.dump("f_out", out_q[:], [c.trk(out_q)])
        emit_tail(core, out_q, c.trk(out_q), xf, c.trk(xf), src_x, trk_srcx,
                  dst_x, trk_dstx, q, xfsem)


def ycol1(h):
    return (h // 3) * 512 + (h % 3) * 129


def emit_b1attn(core, seq, src_x, trk_srcx, dst_x, trk_dstx, st):
    kb, c = core.kb, core
    SB = c.CBASE
    S = seq
    wq = c.at(f"a1wq{S}", [128, CH * 1024], BF16, SB)
    wo = c.at(f"a1wo{S}", [128, CH * 1024], BF16, SB + 16384)
    wkd = c.at(f"a1wkd{S}", [128, 512], BF16, SB + 32768)
    wvd = c.at(f"a1wvd{S}", [128, 512], BF16, SB + 33792)
    wku = c.at(f"a1wku{S}", [64, 128], BF16, SB + 34816)
    wvu = c.at(f"a1wvu{S}", [64, 128], BF16, SB + 35072)
    kf = c.at(f"a1kf{S}", [128, KV1 * T], BF16, SB + 35328)
    Vtm = c.at(f"a1V{S}", [128, 16 * 258], BF16, SB + 43520)
    klat = c.at(f"a1klat{S}", [64, QW], BF16, SB + 51776)
    vlat = c.at(f"a1vlat{S}", [64, QW], BF16, SB + 52800)
    xbf = [c.at(f"a1xbf{S}_{i}", [128, CH * QW], BF16, SB + 53824 + i * 8192)
           for i in range(2)]
    qf = c.at(f"a1qf{S}", [128, CH * QW], BF16, SB + 70208)
    yfm = c.at(f"a1yfm{S}", [128, CH * QW], BF16, SB + 78400)
    ao = c.at(f"a1ao{S}", [128, CH * QW], BF16, SB + 86592)
    xf = c.at(f"a1xf{S}", [128, CH * 256], F32, SB + 94784)
    yn = c.at(f"a1yn{S}", [128, 1024], BF16, SB + 102976)
    probs = [c.at(f"a1pr{S}_{i}", [128, QW], BF16, SB + 105024 + i * 1024)
             for i in range(3)]
    probsT = [c.at(f"a1pT{S}_{i}", [128, QW], BF16, SB + 108096 + i * 1024)
              for i in range(3)]
    vm = [c.at(f"a1vm{S}_{i}", [128, QW], BF16, SB + 111168 + i * 1024)
          for i in range(2)]

    xlsem = [kb.sem(f"a1xl{S}_{i}") for i in range(2)]
    xfsem = kb.sem(f"a1xf{S}")
    kwsem = kb.sem(f"a1kw{S}")

    for (dst, key) in ((wq, "wq1"), (wo, "wo1"), (wkd, "wkd"), (wvd, "wvd")):
        srcw = c.Wsrc[key]
        nb, P, F = srcw.shape
        kb.step("gpsimd", lambda e, d=dst, s=srcw, nb=nb, F=F: e.dma_start(
            d[:, 0:nb * F].rearrange("p (g f) -> p g f", g=nb),
            s.rearrange("g p f -> p g f")),
            writes=[c.trk(dst)], sem=kwsem, amt=16)
    for (dst, key) in ((wku, "wku"), (wvu, "wvu")):
        srcw = c.Wsrc[key]
        for rr in range(2):
            kb.step("gpsimd", lambda e, d=dst, s=srcw, rr=rr: e.dma_start(
                d[32 * rr:32 * rr + 32, :], s[0]),
                writes=[c.trk(dst)], sem=kwsem, amt=16)
    kwtok = (kwsem, kwsem.n)

    rot = [c.ps_a, c.ps_b, c.ps_c]
    sqp = [c.sqs[:, 0:512], c.sqs[:, 512:1024]]

    def mm_group(pbank, prow_sl, pcol_sl, tiles, rhss, reads, extra_waits=()):
        def f(e, tiles=tuple(tiles), rhss=tuple(rhss)):
            n = len(tiles)
            for i in range(n):
                ins = e.matmul(pbank[prow_sl, pcol_sl], tiles[i], rhss[i],
                               start=(i == 0), stop=(i == n - 1))
            return ins
        return kb.step("tensor", f, reads=list(reads), writes=[c.trk(pbank)],
                       waits=list(extra_waits))

    def emit_A1(q, split=False):
        xb = xbf[q % 2]
        txb = c.trk(xb)
        srcv = src_x.rearrange("(c k) t -> k c t", k=128)[:, :, q * QW:(q + 1) * QW]
        kb.step("gpsimd", lambda e, d=xb, s=srcv: e.dma_start(
            d[:].rearrange("k (c t) -> k c t", c=CH), s),
            reads=[trk_srcx], writes=[txb], sem=xlsem[q % 2], amt=16)
        return emit_ln(core, xb, txb, split=split)

    emit_A1(0)
    for q in range(NQ):
        qcols = slice(q * QW, (q + 1) * QW)
        xb = xbf[q % 2]
        txb = c.trk(xb)
        for (lat, wdn) in ((klat, wkd), (vlat, wvd)):
            pb = rot[0] if lat is klat else rot[1]
            mm_group(pb, slice(0, 64), slice(0, QW),
                     [wdn[:, cc * 64:(cc + 1) * 64] for cc in range(CH)],
                     [xb[:, cc * QW:(cc + 1) * QW] for cc in range(CH)],
                     [txb, c.trk(wdn)], extra_waits=[kwtok])
            kb.step("vector", lambda e, lat=lat, pb=pb: e.tensor_copy(
                lat[:], pb[0:64, 0:QW]), reads=[c.trk(pb)], writes=[c.trk(lat)])
        for kv in range(KV1):
            pb = rot[2]
            mm_group(pb, slice(None), slice(0, QW),
                     [wku[32 * kv:32 * kv + 32, :]],
                     [klat[32 * kv:32 * kv + 32, :]],
                     [c.trk(klat), c.trk(wku)], extra_waits=[kwtok])
            dstk = kf[:, kv * T + q * QW: kv * T + (q + 1) * QW]
            kb.step("vector", lambda e, d=dstk, pb=pb: e.tensor_copy(
                d, pb[:, 0:QW]), reads=[c.trk(pb)], writes=[c.trk(kf)])
            pb2 = rot[(kv + 1) % 3]
            mm_group(pb2, slice(None), slice(0, QW),
                     [wvu[32 * kv:32 * kv + 32, :]],
                     [vlat[32 * kv:32 * kv + 32, :]],
                     [c.trk(vlat), c.trk(wvu)], extra_waits=[kwtok])
            vmb = vm[kv % 2]
            kb.step("vector", lambda e, vmb=vmb, pb2=pb2: e.tensor_copy(
                vmb[:], pb2[:, 0:QW]), reads=[c.trk(pb2)], writes=[c.trk(vmb)])
            for tw in range(4):
                kc = 4 * q + tw
                ptb = [c.pt_a, c.pt_b][tw % 2]
                kb.step("tensor", lambda e, vmb=vmb, tw=tw, ptb=ptb: e.transpose(
                    ptb[:, 0:128], vmb[:, tw * 128:(tw + 1) * 128], c.ident[:]),
                    reads=[c.trk(vmb), c.trk(c.ident)], writes=[c.trk(ptb)])
                dstv = Vtm[:, kc * 258 + 129 * kv: kc * 258 + 129 * kv + 128]
                kb.step("vector", lambda e, dstv=dstv, ptb=ptb: e.tensor_copy(
                    dstv, ptb[:, 0:128]), reads=[c.trk(ptb)], writes=[c.trk(Vtm)])
        for tw in range(4):
            kc = 4 * q + tw
            ones_ap = Vtm[:, kc * 258:(kc + 1) * 258].rearrange(
                "p (h d) -> p h d", d=129)[:, :, 128:129]
            kb.step("gpsimd", lambda e, a=ones_ap: e.memset(a, 1.0), writes=[c.trk(Vtm)])
        for kv in range(KV1):
            sq = sqp[kv % 2]
            srck = kf[:, kv * T + q * QW: kv * T + (q + 1) * QW]
            kb.step("scalar", lambda e, sq=sq, s=srck: e.activation(
                sq, s, AF.Square), reads=[c.trk(kf)], writes=[c.trk(c.sqs)])
            prow = rot[kv % 3]
            def knorm(e, sq=sq, prow=prow):
                e.matmul(prow[0:1, 0:QW], c.ones_bf[:, 0:1], sq,
                         start=True, stop=True)
                return e.matmul(prow[0:1, 0:2], c.ident[0:1, 1:2],
                                c.ones_bf[0:1, 0:2], start=False, stop=True,
                                skip_group_check=True)
            kb.step("tensor", knorm, reads=[c.trk(c.sqs), c.trk(c.ones_bf),
                    c.trk(c.ident)], writes=[c.trk(prow)])
            if q == 0:
                kb.step("vector", lambda e, kv=kv, prow=prow: e.reduce_max(
                    c.kmax[0:1, kv:kv + 1], prow[0:1, 0:QW], axis=AX.X),
                    reads=[c.trk(prow)], writes=[c.trk(c.kmax)])
            else:
                kb.step("vector", lambda e, kv=kv, prow=prow: e.reduce_max(
                    c.kmax[0:1, 16 + kv:17 + kv], prow[0:1, 0:QW], axis=AX.X),
                    reads=[c.trk(prow)], writes=[c.trk(c.kmax)])
                kb.step("vector", lambda e, kv=kv: e.tensor_max(
                    c.kmax[0:1, kv:kv + 1], c.kmax[0:1, kv:kv + 1],
                    c.kmax[0:1, 16 + kv:17 + kv]),
                    reads=[c.trk(c.kmax)], writes=[c.trk(c.kmax)])
        kb.step("scalar", lambda e: e.activation(c.kmaxbf[0:1, 0:2],
                c.kmax[0:1, 0:2], AF.Copy), reads=[c.trk(c.kmax)],
                writes=[c.trk(c.kmaxbf)])
        kb.step("tensor", lambda e: e.matmul(c.ps_c[:, 16:18], c.ones_bf[0:1, :],
                c.kmaxbf[0:1, 0:2], start=True, stop=True),
                reads=[c.trk(c.kmaxbf), c.trk(c.ones_bf)], writes=[c.trk(c.ps_c)])
        fl = c.pe_flush()
        kb.step("vector", lambda e: e.tensor_copy(c.kmaxsb[:, 0:2], c.ps_c[:, 16:18]),
                reads=[c.trk(c.ps_c)], writes=[c.trk(c.kmaxsb)], waits=[fl])
        for m in range(CH):
            pb = rot[m % 3]
            mm_group(pb, slice(None), slice(0, QW),
                     [wq[:, m * 1024 + cc * 128: m * 1024 + (cc + 1) * 128]
                      for cc in range(CH)],
                     [xb[:, cc * QW:(cc + 1) * QW] for cc in range(CH)],
                     [txb, c.trk(wq)], extra_waits=[kwtok])
            kb.step("vector", lambda e, m=m, pb=pb: e.tensor_copy(
                qf[:, m * QW:(m + 1) * QW], pb[:, 0:QW]),
                reads=[c.trk(pb)], writes=[c.trk(qf)])
        pcnt = [0]
        for qq in range(4):
            i_qt = 4 * q + qq
            klen = (i_qt + 1) * 128
            first_nm = [True]
            for h in range(H1):
                sq = sqp[h % 2]
                srcq = qf[:, h * QW + qq * 128: h * QW + (qq + 1) * 128]
                kb.step("gpsimd", lambda e, sq=sq, s=srcq: e.tensor_mul(
                    sq[:, 0:128], s, s), reads=[c.trk(qf)],
                    writes=[c.trk(c.sqs)])
                kb.step("tensor", lambda e, h=h, sq=sq, f=first_nm[0]: e.matmul(
                    c.ps_c[:, h:h + 1], sq[:, 0:128], c.ones_bf[:, 0:1],
                    start=f, stop=(h == H1 - 1), skip_group_check=True),
                    reads=[c.trk(c.sqs), c.trk(c.ones_bf)], writes=[c.trk(c.ps_c)])
                first_nm[0] = False
            fl2 = c.pe_flush()
            kb.step("vector", lambda e: e.tensor_copy(c.bias3[:, 0:8], c.ps_c[:, 0:8]),
                    reads=[c.trk(c.ps_c)], writes=[c.trk(c.bias3)], waits=[fl2])
            for kv in range(KV1):
                kb.step("vector", lambda e, kv=kv: e.tensor_scalar(
                    c.bias3[:, 16 + 4 * kv:20 + 4 * kv], c.bias3[:, 4 * kv:4 * kv + 4],
                    c.kmaxsb[:, kv:kv + 1], None, ALU.mult),
                    reads=[c.trk(c.bias3), c.trk(c.kmaxsb)], writes=[c.trk(c.bias3)])
            kb.step("scalar", lambda e: e.activation(c.bias3[:, 32:40],
                    c.bias3[:, 16:24], AF.Sqrt), reads=[c.trk(c.bias3)],
                    writes=[c.trk(c.bias3)])
            kb.step("vector", lambda e: e.tensor_scalar(c.bias3[:, 32:40],
                    c.bias3[:, 32:40], -SCALE1, None, ALU.mult),
                    reads=[c.trk(c.bias3)], writes=[c.trk(c.bias3)])
            nw = ceil_div(klen, QW)
            units = [(h, w) for h in range(H1) for w in range(nw)]

            def emit_sc(h, w):
                L = min(QW, klen - w * QW)
                kv = h // 4
                par = pcnt[0] % 3
                pcnt[0] += 1
                sc = [c.ps_a, c.ps_b, c.ps_c][par]
                pr = probs[par]
                q_l = qf[:, h * QW + qq * 128: h * QW + (qq + 1) * 128]
                k_l = kf[:, kv * T + w * QW: kv * T + w * QW + L]
                def scmm(e, sc=sc, q_l=q_l, k_l=k_l, L=L):
                    ins = e.matmul(sc[:, 0:L], q_l, k_l, start=True, stop=True)
                    if L < 384:
                        ins = e.matmul(sc[0:1, 0:2], c.ident[0:1, 1:2],
                                       c.ones_bf[0:1, 0:2], start=False, stop=True,
                                       skip_group_check=True)
                    return ins
                kb.step("tensor", scmm, reads=[c.trk(qf), c.trk(kf)],
                        writes=[c.trk(sc)])
                kb.step("scalar", lambda e, sc=sc, pr=pr, h=h, L=L: e.activation(
                    pr[:, 0:L], sc[:, 0:L], AF.Exp,
                    bias=c.bias3[:, 32 + h:33 + h], scale=SCALE1),
                    reads=[c.trk(sc), c.trk(c.bias3)], writes=[c.trk(pr)])
                if w == nw - 1:
                    kb.step("gpsimd", lambda e, pr=pr, L=L: e.affine_select(
                        out=pr[:, L - 128:L], in_=pr[:, L - 128:L],
                        compare_op=ALU.is_ge, fill=0.0, base=0,
                        pattern=[[-1, 128]], channel_multiplier=1),
                        reads=[c.trk(pr)], writes=[c.trk(pr)])
                return par

            def emit_tr(h, w, par):
                L = min(QW, klen - w * QW)
                pr = probs[par]
                pT = probsT[par]
                ptp = [c.pt_a, c.pt_b][par % 2]
                nch = ceil_div(L, 128)
                def trgrp(e, pr=pr, ptp=ptp, nch=nch):
                    for tch in range(nch):
                        ins = e.transpose(ptp[:, tch * 128:(tch + 1) * 128],
                                          pr[:, tch * 128:(tch + 1) * 128], c.ident[:])
                    return ins
                kb.step("tensor", trgrp, reads=[c.trk(pr), c.trk(c.ident)],
                        writes=[c.trk(ptp)])
                kb.step("vector", lambda e, pT=pT, ptp=ptp, L=L: e.tensor_copy(
                    pT[:, 0:L], ptp[:, 0:L]), reads=[c.trk(ptp)], writes=[c.trk(pT)])

            def emit_pv(h, w, par):
                L = min(QW, klen - w * QW)
                kv = h // 4
                pT = probsT[par]
                nch = ceil_div(L, 128)
                yc = ycol1(h)
                def pvgrp(e, pT=pT, h=h, kv=kv, w=w, nch=nch, yc=yc,
                          first=(w == 0), last=(w == nw - 1)):
                    for tch in range(nch):
                        kc = 4 * w + tch
                        ins = e.matmul(c.ps_y[:, yc:yc + 129],
                                       pT[:, tch * 128:(tch + 1) * 128],
                                       Vtm[:, kc * 258 + 129 * kv: kc * 258 + 129 * (kv + 1)],
                                       start=(first and tch == 0),
                                       stop=(last and tch == nch - 1))
                    return ins
                kb.step("tensor", pvgrp, reads=[c.trk(pT), c.trk(Vtm)],
                        writes=[c.trk(c.ps_y)])

            plist = []
            for u in units:
                par = emit_sc(*u)
                plist.append((u[0], u[1], par))
                if len(plist) >= 2:
                    emit_tr(*plist[-2])
                if len(plist) >= 3:
                    emit_pv(*plist[-3])
            emit_tr(*plist[-1])
            if len(plist) >= 2:
                emit_pv(*plist[-2])
            emit_pv(*plist[-1])
            fl3 = c.pe_flush()
            for g, ng in ((0, 3), (1, 3), (2, 2)):
                kb.step("vector", lambda e, g=g, ng=ng: e.reciprocal(
                    c.recips[:, 3 * g:3 * g + ng],
                    bass.AP(c.ps_y, 512 * g + 128, [[1536, 128], [129, ng]])),
                    reads=[c.trk(c.ps_y)], writes=[c.trk(c.recips)], waits=[fl3])
            for h in range(H1):
                yc = ycol1(h)
                kb.step("scalar", lambda e, h=h, yc=yc: e.mul(
                    yn[:, 128 * h:128 * (h + 1)], c.ps_y[:, yc:yc + 128],
                    c.recips[:, h:h + 1]),
                    reads=[c.trk(c.ps_y), c.trk(c.recips)], writes=[c.trk(yn)])
            for half in range(2):
                ptp = [c.pt_a, c.pt_b][half]
                def ytr(e, half=half, ptp=ptp):
                    for c2 in range(4):
                        ins = e.transpose(ptp[:, c2 * 128:(c2 + 1) * 128],
                                          yn[:, (half * 4 + c2) * 128:(half * 4 + c2 + 1) * 128],
                                          c.ident[:])
                    return ins
                kb.step("tensor", ytr, reads=[c.trk(yn), c.trk(c.ident)], writes=[c.trk(ptp)])
                dstv = yfm[:].rearrange("p (c t) -> p c t", c=CH)[
                    :, half * 4:(half + 1) * 4, qq * 128:(qq + 1) * 128]
                kb.step("vector", lambda e, ptp=ptp, dstv=dstv: e.tensor_copy(
                    dstv, ptp[:].rearrange("p (c t) -> p c t", c=4)),
                    reads=[c.trk(ptp)], writes=[c.trk(yfm)])
        fin1 = emit_A1(q + 1, split=True) if q + 1 < NQ else None
        for m in range(CH):
            pb = rot[m % 3]
            mm_group(pb, slice(None), slice(0, QW),
                     [wo[:, m * 1024 + cc * 128: m * 1024 + (cc + 1) * 128]
                      for cc in range(CH)],
                     [yfm[:, cc * QW:(cc + 1) * QW] for cc in range(CH)],
                     [c.trk(yfm), c.trk(wo)], extra_waits=[kwtok])
            kb.step("vector", lambda e, m=m, pb=pb: e.tensor_copy(
                ao[:, m * QW:(m + 1) * QW], pb[:, 0:QW]),
                reads=[c.trk(pb)], writes=[c.trk(ao)])
        if fin1 is not None:
            fin1()
        emit_tail(core, ao, c.trk(ao), xf, c.trk(xf), src_x, trk_srcx,
                  dst_x, trk_dstx, q, xfsem)


def emit_b1ffn(core, seq, src_x, trk_srcx, dst_x, trk_dstx, st):
    kb, c = core.kb, core
    SB = c.CBASE
    S = seq
    kern = c.at(f"f1k{S}", [128, CH * 1024], F32, SB)
    kq = c.at(f"f1kq{S}", [128, CH * 1024], BF16, SB + 32768)
    la = c.at(f"f1la{S}", [128, 128], BF16, SB + 49152)
    lb = c.at(f"f1lb{S}", [16, CH * 128], BF16, SB + 49408)
    biasb = c.at(f"f1bias{S}", [128, 8], F32, SB + 51456)
    qcols_t = c.at(f"f1qc{S}", [128, 16], F32, SB + 51488)
    xbf = [c.at(f"f1xbf{S}_{i}", [128, CH * QW], BF16, SB + 51552 + i * 8192)
           for i in range(2)]
    xa = c.at(f"f1xa{S}", [16, QW], BF16, SB + 67936)
    out_q = c.at(f"f1out{S}", [128, CH * QW], BF16, SB + 68960)
    outs = c.at(f"f1outs{S}", [128, CH * QW], BF16, SB + 77152)
    xf = c.at(f"f1xf{S}", [128, CH * 256], F32, SB + 85344)
    xlsem = [kb.sem(f"f1xl{S}_{i}") for i in range(2)]
    xfsem = kb.sem(f"f1xf{S}")
    kwsem = kb.sem(f"f1kw{S}")

    rot = [c.ps_a, c.ps_b, c.ps_c]

    srck = c.Wsrc["kern"]
    kb.step("sync", lambda e: e.dma_start(
        kern[:].rearrange("p (g f) -> p g f", g=CH),
        srck.rearrange("g p f -> p g f")), writes=[c.trk(kern)], sem=kwsem, amt=16)
    kb.step("gpsimd", lambda e: e.dma_start(la[:], c.Wsrc["la"][0]),
            writes=[c.trk(la)], sem=kwsem, amt=16)
    kb.step("gpsimd", lambda e: e.dma_start(
        lb[:].rearrange("p (g f) -> p g f", g=CH),
        c.Wsrc["lb"].rearrange("g p f -> p g f")), writes=[c.trk(lb)], sem=kwsem, amt=16)
    kb.step("gpsimd", lambda e: e.dma_start(biasb[:], c.Wsrc["bias1"]),
        writes=[c.trk(biasb)], sem=kwsem, amt=16)
    kwtok = (kwsem, kwsem.n)

    for m in range(CH):
        kb.step("vector", lambda e, m=m: e.reduce_max(
            c.bias3[:, m:m + 1], kern[:, m * 1024:(m + 1) * 1024], axis=AX.X,
            apply_absolute_value=True),
            reads=[c.trk(kern)], writes=[c.trk(c.bias3)], waits=[kwtok])
    kb.step("vector", lambda e: e.reduce_max(
        c.bias3[:, 8:9], c.bias3[:, 0:8], axis=AX.X),
        reads=[c.trk(c.bias3)], writes=[c.trk(c.bias3)])
    kb.step("tensor", lambda e: e.transpose(c.ps_c[0:1, 0:128],
            c.bias3[:, 8:9], c.identf[:]),
            reads=[c.trk(c.bias3), c.trk(c.identf)], writes=[c.trk(c.ps_c)])
    fl = c.pe_flush()
    kb.step("vector", lambda e: e.reduce_max(c.rows_c[0:1, 0:1], c.ps_c[0:1, 0:128],
            axis=AX.X), reads=[c.trk(c.ps_c)], writes=[c.trk(c.rows_c)], waits=[fl])
    kb.step("vector", lambda e: e.reciprocal(c.rows_c[0:1, 1:2], c.rows_c[0:1, 0:1]),
            reads=[c.trk(c.rows_c)], writes=[c.trk(c.rows_c)])
    kb.step("vector", lambda e: e.tensor_scalar(c.rows_c[0:1, 2:3], c.rows_c[0:1, 1:2],
            7.0, None, ALU.mult), reads=[c.trk(c.rows_c)], writes=[c.trk(c.rows_c)])
    kb.step("vector", lambda e: e.tensor_scalar(c.rows_c[0:1, 3:4], c.rows_c[0:1, 0:1],
            1.0 / 7.0, None, ALU.mult), reads=[c.trk(c.rows_c)], writes=[c.trk(c.rows_c)])
    kb.step("tensor", lambda e: e.matmul(c.ps_c[:, 16:18], c.onesf[0:1, :],
            c.rows_c[0:1, 2:4], start=True, stop=True),
            reads=[c.trk(c.rows_c), c.trk(c.onesf)], writes=[c.trk(c.ps_c)])
    fl2 = c.pe_flush()
    kb.step("vector", lambda e: e.tensor_copy(qcols_t[:, 0:2], c.ps_c[:, 16:18]),
            reads=[c.trk(c.ps_c)], writes=[c.trk(qcols_t)], waits=[fl2])
    for m in range(CH):
        sl = slice(m * 1024, (m + 1) * 1024)
        kb.step("vector", lambda e, sl=sl: e.tensor_scalar(
            kern[:, sl], kern[:, sl], qcols_t[:, 0:1], None, ALU.mult),
            reads=[c.trk(kern), c.trk(qcols_t)], writes=[c.trk(kern)])
        kb.step("vector", lambda e, sl=sl: e.tensor_scalar(
            kern[:, sl], kern[:, sl], MAGIC, MAGIC, ALU.add, ALU.subtract),
            reads=[c.trk(kern)], writes=[c.trk(kern)])
        kb.step("vector", lambda e, sl=sl: e.tensor_scalar(
            kern[:, sl], kern[:, sl], 7.0, None, ALU.min),
            reads=[c.trk(kern)], writes=[c.trk(kern)])
        kb.step("vector", lambda e, sl=sl: e.tensor_scalar(
            kq[:, sl], kern[:, sl], -8.0, None, ALU.max),
            reads=[c.trk(kern)], writes=[c.trk(kq)])

    for q in range(NQ):
        xb = xbf[q % 2]
        txb = c.trk(xb)
        srcv = src_x.rearrange("(c k) t -> k c t", k=128)[:, :, q * QW:(q + 1) * QW]
        kb.step("gpsimd", lambda e, d=xb, s=srcv: e.dma_start(
            d[:].rearrange("k (c t) -> k c t", c=CH), s),
            reads=[trk_srcx], writes=[txb], sem=xlsem[q % 2], amt=16)
        emit_ln(core, xb, txb)
        def xamm(e, xb=xb):
            for cc in range(CH):
                e.matmul(c.ps_c[0:16, 0:QW], la[:, cc * 16:(cc + 1) * 16],
                         xb[:, cc * QW:(cc + 1) * QW],
                         start=(cc == 0), stop=(cc == CH - 1))
            return e.matmul(c.ps_c[0:1, 508:510], c.ident[0:1, 1:2],
                            c.ones_bf[0:1, 0:2], start=True, stop=True,
                            skip_group_check=True)
        kb.step("tensor", xamm, reads=[txb, c.trk(la)], writes=[c.trk(c.ps_c)],
                waits=[kwtok])
        kb.step("vector", lambda e: e.tensor_copy(xa[:], c.ps_c[0:16, 0:QW]),
                reads=[c.trk(c.ps_c)], writes=[c.trk(xa)])
        for m in range(CH):
            pb = rot[m % 3]
            plor = c.ps_y
            def mainmm(e, m=m, pb=pb, xb=xb):
                for cc in range(CH):
                    ins = e.matmul(pb[:, 0:QW],
                                   kq[:, m * 1024 + cc * 128: m * 1024 + (cc + 1) * 128],
                                   xb[:, cc * QW:(cc + 1) * QW],
                                   start=(cc == 0), stop=(cc == CH - 1))
                return ins
            kb.step("tensor", mainmm, reads=[txb, c.trk(kq)], writes=[c.trk(pb)])
            def loramm(e, m=m, plor=plor):
                return e.matmul(plor[:, 0:QW], lb[:, m * 128:(m + 1) * 128], xa[:],
                                start=True, stop=True)
            kb.step("tensor", loramm, reads=[c.trk(xa), c.trk(lb)], writes=[c.trk(plor)])
            kb.step("vector", lambda e, m=m, pb=pb: e.tensor_scalar(
                outs[:, m * QW:(m + 1) * QW], pb[:, 0:QW], qcols_t[:, 1:2],
                biasb[:, m:m + 1], ALU.mult, ALU.add),
                reads=[c.trk(pb), c.trk(qcols_t), c.trk(biasb)], writes=[c.trk(outs)])
            kb.step("vector", lambda e, m=m, plor=plor: e.tensor_tensor(
                out_q[:, m * QW:(m + 1) * QW], outs[:, m * QW:(m + 1) * QW],
                plor[:, 0:QW], ALU.add),
                reads=[c.trk(outs), c.trk(plor)], writes=[c.trk(out_q)])
        emit_tail(core, out_q, c.trk(out_q), xf, c.trk(xf), src_x, trk_srcx,
                  dst_x, trk_dstx, q, xfsem)


# version 09
# speedup vs baseline: 1.1759x; 1.1759x over previous
import math
from contextlib import ExitStack

import numpy as np
import ml_dtypes
import concourse.bass as bass
import concourse.mybir as mybir
from concourse.bass_utils import run_bass_kernel_spmd

F32 = mybir.dt.float32
BF16 = mybir.dt.bfloat16
I32 = mybir.dt.int32
ALU = mybir.AluOpType
AF = mybir.ActivationFunctionType
AX = mybir.AxisListType

T = 2048
NQ = 4
QW = 512
CH = 8
NSEQ = 2
H0, HS0, DHR = 16, 64, 32
SCALE0 = 1.0 / math.sqrt(HS0 + DHR)
H1, KV1, DV1, DLAT = 8, 2, 128, 32
SCALE1 = 1.0 / math.sqrt(DV1)
FFH = 3072
FFC = FFH // 128
EPS = 1e-5
MAGIC = 12582912.0
TWO_PI = 2 * math.pi
PI = math.pi
HI2PI = float(np.float32(6.28125))
LO2PI = float(np.float32(TWO_PI) - np.float32(6.28125))


def pack_w(W, mw=128):
    W = np.ascontiguousarray(W)
    K, M = W.shape
    assert M % mw == 0
    nb = M // mw
    if K >= 128:
        assert K % 128 == 0
        kc = K // 128
        Wp = W.reshape(kc, 128, nb, mw).transpose(2, 1, 0, 3).reshape(nb, 128, kc * mw)
    else:
        Wp = W.reshape(1, K, nb, mw).transpose(2, 1, 0, 3).reshape(nb, K, mw)
    return np.ascontiguousarray(Wp)


def prep_host(inputs):
    d = {}
    d["wdq"] = pack_w(np.asarray(inputs["L0_attn_W_DQ"]))
    d["wuq"] = pack_w(np.asarray(inputs["L0_attn_W_UQ"]))
    d["wqr"] = pack_w(np.asarray(inputs["L0_attn_W_QR"]))
    d["wdkv"] = pack_w(np.asarray(inputs["L0_attn_W_DKV"]))
    d["wuk"] = pack_w(np.asarray(inputs["L0_attn_W_UK"]))
    d["wuv"] = pack_w(np.asarray(inputs["L0_attn_W_UV"]))
    d["wkr"] = pack_w(np.asarray(inputs["L0_attn_W_KR"]), mw=32)
    d["wo0"] = pack_w(np.asarray(inputs["L0_attn_W_O"]))
    d["wg"] = pack_w(np.ascontiguousarray(np.asarray(inputs["L0_ffn_w_gate"]).T))
    d["wu"] = pack_w(np.ascontiguousarray(np.asarray(inputs["L0_ffn_w_up"]).T))
    d["wd"] = pack_w(np.ascontiguousarray(np.asarray(inputs["L0_ffn_w_down"]).T))
    d["wq1"] = pack_w(np.asarray(inputs["L1_attn_wq"]))
    d["wkd"] = pack_w(np.asarray(inputs["L1_attn_wk_down"]), mw=64)
    d["wvd"] = pack_w(np.asarray(inputs["L1_attn_wv_down"]), mw=64)
    d["wku"] = pack_w(np.asarray(inputs["L1_attn_wk_up"]))
    d["wvu"] = pack_w(np.asarray(inputs["L1_attn_wv_up"]))
    d["wo1"] = pack_w(np.asarray(inputs["L1_attn_wo"]))
    d["kern"] = pack_w(np.asarray(inputs["L1_ffn_kernel"]))
    d["bias1"] = np.ascontiguousarray(np.asarray(inputs["L1_ffn_bias"]).reshape(8, 128).T)
    d["la"] = pack_w(np.asarray(inputs["L1_ffn_a"]), mw=16)
    d["lb"] = pack_w(np.asarray(inputs["L1_ffn_b"]))
    x_fm = np.ascontiguousarray(np.asarray(inputs["L0_attn_x"]).transpose(0, 2, 1))
    return d, x_fm


class Sem:
    def __init__(self, nc, name):
        self.h = nc.alloc_semaphore(name)
        self.n = 0

    def bump(self, amt):
        self.n += amt
        return self.n


class Trk:
    def __init__(self):
        self.w = {}
        self.r = {}

    def rwait(self):
        return list(self.w.items())

    def wwait(self):
        return list(self.w.items()) + list(self.r.items())

    def setw(self, tok):
        self.w[tok[0]] = max(self.w.get(tok[0], 0), tok[1])

    def addr(self, tok):
        self.r[tok[0]] = max(self.r.get(tok[0], 0), tok[1])


ENGS = ("sync", "scalar", "tensor", "vector", "gpsimd")


class Kb:
    def __init__(self, nc):
        self.nc = nc
        self.items = {e: [] for e in ENGS}
        self.esem = {e: Sem(nc, f"ms_{e}") for e in ENGS}
        self.nsteps = 0

    def sem(self, name):
        return Sem(self.nc, name)

    def step(self, eng, fn, reads=(), writes=(), waits=(), sem="auto", amt=1):
        s = self.esem[eng] if sem == "auto" else sem
        tgt = s.bump(amt)
        tok = (s, tgt)
        allw = list(waits) + list(getattr(self, "global_waits", []))
        for t in reads:
            allw += t.rwait()
        for t in writes:
            allw += t.wwait()
        for t in reads:
            t.addr(tok)
        for t in writes:
            t.setw(tok)
        wl = [(ss.h, v) for ss, v in allw if v > 0]
        self.items[eng].append((wl, fn, s.h, amt))
        self.nsteps += 1
        return tok

    def barrier_token(self):
        return [(s, s.n) for s in self.esem.values()]

    def finalize(self, block):
        own = {e: self.esem[e].h for e in ENGS}
        for e in ENGS:
            items = self.items[e]
            ownh = own[e]
            candrain = e in ("vector", "scalar", "gpsimd")

            def body(eng, items=items, ownh=ownh, candrain=candrain):
                waited = {}
                hist = [0, 0, 0]
                for wl, fn, semh, amt in items:
                    if candrain and any(h is ownh and v > hist[-3] for h, v in wl):
                        eng.drain()
                    for h, v in wl:
                        if h is ownh:
                            continue
                        if waited.get(h, -1) >= v:
                            continue
                        waited[h] = v
                        eng.wait_ge(h, v)
                    ins = fn(eng)
                    ins.then_inc(semh, amt)
                    hist.append(hist[-1] + (amt if semh is ownh else 0))
            getattr(block, e)(body)


class WStream:
    def __init__(self, kb, name, src_ap, nb, P, F, sbuf_at, batch=1, nbuf=2,
                 share=None):
        self.kb = kb
        self.name = name
        self.src = src_ap
        self.nb, self.P, self.F, self.batch = nb, P, F, batch
        nc = kb.nc
        self.stash = nc.dram_tensor(f"st_{name}", [nb, P, F], BF16, kind="Internal").ap()
        self.nbuf = nbuf
        self.sems = [kb.sem(f"wl_{name}{i}") for i in range(nbuf)]
        if share is not None:
            self.bufs = share.bufs
            self.trks = share.trks
        else:
            self.bufs = []
            self.trks = []
            for i in range(nbuf):
                t = nc.alloc_sbuf_tensor_at(f"wb_{name}{i}", [P, batch * F], BF16,
                                            offset=sbuf_at + i * batch * F * 2)
                self.bufs.append(t)
                self.trks.append(Trk())
        self.stash_tok = {}
        self.cnt = 0

    def load(self, bi):
        kb = self.kb
        slot = self.cnt % self.nbuf
        self.cnt += 1
        buf, trk, sem = self.bufs[slot], self.trks[slot], self.sems[slot]
        g = min(self.batch, self.nb - bi * self.batch)
        first = bi not in self.stash_tok
        bufv = buf[:, 0:g * self.F].rearrange("p (g f) -> p g f", g=g)
        if first:
            src = self.src[bi * self.batch: bi * self.batch + g].rearrange(
                "g p f -> p g f")
            kb.step("gpsimd", lambda e, d=bufv, s=src: e.dma_start(d, s),
                    writes=[trk], sem=sem, amt=16)
            sdst = self.stash[bi * self.batch: bi * self.batch + g].rearrange(
                "g p f -> p g f")
            tok = kb.step("gpsimd", lambda e, d=sdst, s=bufv: e.dma_start(d, s),
                          reads=[trk], waits=[(sem, sem.n)], sem=sem, amt=16)
            self.stash_tok[bi] = tok
        else:
            stok = self.stash_tok[bi]
            src = self.stash[bi * self.batch: bi * self.batch + g].rearrange(
                "g p f -> p g f")
            kb.step("sync", lambda e, d=bufv, s=src: e.dma_start(d, s),
                    writes=[trk], waits=[stok], sem=sem, amt=16)
        return (buf, trk, (sem, sem.n))

    def tile(self, h, blk_in_batch, kc, mw):
        buf = h[0]
        return buf[:, blk_in_batch * self.F + kc * mw:(blk_in_batch * self.F) + kc * mw + mw]


def ceil_div(a, b):
    return (a + b - 1) // b


def ycol0(h):
    return (h // 7) * 512 + (h % 7) * 65


class Core:
    def __init__(self, nc, kb):
        self.nc = nc
        self.kb = kb
        self.MBASE = 16512
        self.cur = self.MBASE
        self.trks = {}

    def at(self, name, shape, dtype, off):
        t = self.nc.alloc_sbuf_tensor_at(name, shape, dtype, offset=self.MBASE + off)
        self.trks[t.name] = Trk()
        return t

    def trk(self, t):
        return self.trks[t.name if hasattr(t, "name") else t]

    def pe_flush(self, reads=()):
        return self.kb.step("tensor", lambda e: e.matmul(
            self.ps_c[0:1, 48:50], self.ones_bf[0:1, 0:1], self.ones_bf[0:1, 0:2],
            start=True, stop=True, skip_group_check=True), reads=list(reads),
            writes=[self.trk(self.ps_c)])

    def dump(self, name, ap, trks):
        dbg = getattr(self, "dbg", False)
        if not dbg or name in getattr(self, "dumped", {}):
            return
        if isinstance(dbg, (set, frozenset, list, tuple)) and name not in dbg:
            return
        self.dumped = getattr(self, "dumped", {})
        self.dumped[name] = True
        d = self.nc.dram_tensor(f"dbg_{name}", list(ap.shape), ap.dtype,
                                kind="ExternalOutput").ap()
        s = self.kb.sem(f"dbg_{name}")
        self.kb.step("sync", lambda e: e.dma_start(d, ap), reads=list(trks),
                     sem=s, amt=16)
        self.kb.dbg_sems = getattr(self.kb, "dbg_sems", [])
        self.kb.dbg_sems.append((s, s.n))


def setup_consts(core):
    nc, kb = core.nc, core.kb
    c = core
    c.rope4 = c.at("rope4", [128, 2048], F32, 0)
    c.ident = c.at("ident", [128, 128], BF16, 8192)
    c.ones_bf = c.at("ones_bf", [128, 128], BF16, 8448)
    c.epsc = c.at("epsc", [128, 8], F32, 8704)
    c.rows_a = c.at("rows_a", [1, 512], F32, 8736)
    c.rows_b = c.at("rows_b", [1, 512], F32, 10784)
    c.rows_rstd = c.at("rows_rstd", [1, 512], BF16, 12832)
    c.rows_mr = c.at("rows_mr", [1, 512], BF16, 13856)
    c.qn2 = c.at("qn2", [16, 512], F32, 14880)
    c.kmax = c.at("kmax", [1, 32], F32, 16928)
    c.bias3 = c.at("bias3", [128, 48], F32, 17056)
    c.recips = c.at("recips", [128, 16], F32, 17248)
    c.sqs = c.at("sqs", [128, 1024], BF16, 17312)
    c.identf = c.at("identf", [128, 128], F32, 19360)
    c.onesf = c.at("onesf", [128, 128], F32, 19872)
    c.rows_c = c.at("rows_c", [1, 512], F32, 20384)
    c.kmaxbf = c.at("kmaxbf", [1, 32], BF16, 22400)
    c.kmaxsb = c.at("kmaxsb", [128, 16], F32, 22496)
    c.bcR_sb = c.at("bcR_sb", [128, 512], BF16, 22560)
    c.bcM_sb = c.at("bcM_sb", [128, 512], BF16, 23584)
    c.CBASE2 = None
    c.sel64 = c.at("sel64", [128, 2], BF16, 22432)
    c.sel32 = c.at("sel32", [128, 4], BF16, 22464)
    c.CBASE = 24608

    c.ps_a = nc.alloc_psum_tensor("ps_a", [128, 512], F32)
    c.ps_b = nc.alloc_psum_tensor("ps_b", [128, 512], F32)
    c.ps_c = nc.alloc_psum_tensor("ps_c", [128, 512], F32)
    c.ps_y = nc.alloc_psum_tensor("ps_y", [128, 1536], F32)
    c.pt_a = nc.alloc_psum_tensor("pt_a", [128, 512], BF16)
    c.pt_b = nc.alloc_psum_tensor("pt_b", [128, 512], BF16)
    for t in (c.ps_a, c.ps_b, c.ps_c, c.ps_y, c.pt_a, c.pt_b):
        c.trks[t.name] = Trk()

    kb.step("gpsimd", lambda e: e.memset(c.ident[:], 0.0), writes=[c.trk(c.ident)])
    kb.step("gpsimd", lambda e: e.affine_select(
        out=c.ident[:], in_=c.ident[:], compare_op=ALU.not_equal, fill=1.0,
        base=0, pattern=[[-1, 128]], channel_multiplier=1), writes=[c.trk(c.ident)])
    kb.step("gpsimd", lambda e: e.memset(c.ones_bf[:], 1.0), writes=[c.trk(c.ones_bf)])
    kb.step("gpsimd", lambda e: e.memset(c.identf[:], 0.0), writes=[c.trk(c.identf)])
    kb.step("gpsimd", lambda e: e.memset(c.onesf[:], 1.0), writes=[c.trk(c.onesf)])
    for (tt, nb, bw) in ((c.sel64, 2, 64), (c.sel32, 4, 32)):
        kb.step("gpsimd", lambda e, tt=tt: e.memset(tt[:], 1.0), writes=[c.trk(tt)])
        for cc in range(nb):
            kb.step("gpsimd", lambda e, tt=tt, cc=cc, bw=bw: e.affine_select(
                out=tt[:, cc:cc + 1], in_=tt[:, cc:cc + 1], compare_op=ALU.is_ge,
                fill=0.0, base=-bw * cc, pattern=[[0, 1]], channel_multiplier=1),
                reads=[c.trk(tt)], writes=[c.trk(tt)])
            kb.step("gpsimd", lambda e, tt=tt, cc=cc, bw=bw: e.affine_select(
                out=tt[:, cc:cc + 1], in_=tt[:, cc:cc + 1], compare_op=ALU.is_ge,
                fill=0.0, base=(bw * cc + bw - 1), pattern=[[0, 1]],
                channel_multiplier=-1),
                reads=[c.trk(tt)], writes=[c.trk(tt)])
    kb.step("gpsimd", lambda e: e.affine_select(
        out=c.identf[:], in_=c.identf[:], compare_op=ALU.not_equal, fill=1.0,
        base=0, pattern=[[-1, 128]], channel_multiplier=1), writes=[c.trk(c.identf)])
    kb.step("gpsimd", lambda e: e.memset(c.epsc[:], EPS), writes=[c.trk(c.epsc)])

    SC = c.CBASE
    t16 = c.at("rp_t16", [16, T], F32, SC)
    fr = c.at("rp_fr", [16, T], F32, SC + 8192)
    u = c.at("rp_u", [16, T], F32, SC + 16384)
    red = c.at("rp_red", [16, T], F32, SC + 24576)
    sn = c.at("rp_sn", [16, T], F32, SC + 32768)
    comb = c.at("rp_comb", [16, T], F32, SC + 40960)
    jcol = c.at("rp_jcol", [16, 1], F32, SC + 49152)
    inv = c.at("rp_inv", [16, 1], F32, SC + 49184)
    for nm in ("rp_t16", "rp_fr", "rp_u", "rp_red", "rp_sn", "rp_comb", "rp_jcol", "rp_inv"):
        pass
    kb.step("gpsimd", lambda e: e.iota(t16[:], pattern=[[1, T]], base=0,
            channel_multiplier=0, allow_small_or_imprecise_dtypes=True),
            writes=[c.trk(t16)])
    kb.step("gpsimd", lambda e: e.iota(jcol[:], pattern=[[0, 1]], base=0,
            channel_multiplier=2, allow_small_or_imprecise_dtypes=True),
            writes=[c.trk(jcol)])
    kb.step("scalar", lambda e: e.activation(inv[:], jcol[:], AF.Exp,
            scale=-math.log(10000.0) / 32.0), reads=[c.trk(jcol)], writes=[c.trk(inv)])
    kb.step("vector", lambda e: e.tensor_scalar(fr[:], t16[:], inv[:], None, ALU.mult),
            reads=[c.trk(t16), c.trk(inv)], writes=[c.trk(fr)])

    def reduce_to_pi(shift):
        if shift:
            kb.step("vector", lambda e: e.tensor_scalar(u[:], fr[:], 1.0 / TWO_PI,
                    shift / TWO_PI, ALU.mult, ALU.add),
                    reads=[c.trk(fr)], writes=[c.trk(u)])
        else:
            kb.step("vector", lambda e: e.tensor_scalar(u[:], fr[:], 1.0 / TWO_PI,
                    None, ALU.mult), reads=[c.trk(fr)], writes=[c.trk(u)])
        kb.step("vector", lambda e: e.tensor_scalar(u[:], u[:], MAGIC, MAGIC,
                ALU.add, ALU.subtract), reads=[c.trk(u)], writes=[c.trk(u)])
        kb.step("vector", lambda e: e.scalar_tensor_tensor(red[:], u[:], -HI2PI, fr[:],
                ALU.mult, ALU.add), reads=[c.trk(u), c.trk(fr)], writes=[c.trk(red)])
        if shift:
            kb.step("vector", lambda e: e.tensor_scalar(red[:], red[:],
                    float(np.float32(shift)), None, ALU.add),
                    reads=[c.trk(red)], writes=[c.trk(red)])
        kb.step("vector", lambda e: e.scalar_tensor_tensor(red[:], u[:], -LO2PI, red[:],
                ALU.mult, ALU.add), reads=[c.trk(u), c.trk(red)], writes=[c.trk(red)])
        kb.step("vector", lambda e: e.tensor_scalar(red[:], red[:], PI, None, ALU.min),
                reads=[c.trk(red)], writes=[c.trk(red)])
        kb.step("vector", lambda e: e.tensor_scalar(red[:], red[:], -PI, None, ALU.max),
                reads=[c.trk(red)], writes=[c.trk(red)])

    reduce_to_pi(0.0)
    kb.step("scalar", lambda e: e.activation(sn[:], red[:], AF.Sin),
            reads=[c.trk(red)], writes=[c.trk(sn)])
    reduce_to_pi(PI / 2)
    kb.step("scalar", lambda e: e.activation(comb[:], red[:], AF.Sin),
            reads=[c.trk(red)], writes=[c.trk(comb)])
    kb.step("vector", lambda e: e.tensor_add(comb[:], comb[:], sn[:]),
            reads=[c.trk(comb), c.trk(sn)], writes=[c.trk(comb)])
    rsem = kb.sem("rope_rep")
    for k in range(8):
        kb.step("gpsimd", lambda e, k=k: e.dma_start(
            c.rope4[16 * k:16 * k + 16, :], comb[:]),
            reads=[c.trk(comb)], writes=[c.trk(c.rope4)], sem=rsem, amt=16)
    kb.rope_tok = (rsem, rsem.n)
    core.kb.global_waits = kb.barrier_token() + [kb.rope_tok]


def emit_ln(core, xbf, trk_x, ncols=QW, nch=CH, outs=None, trk_out=None):
    kb, c = core.kb, core
    pa, pb = c.ps_a, c.ps_b
    sq_par = [c.sqs[:, 0:512], c.sqs[:, 512:1024]]
    def s1(e):
        for ch in range(nch):
            i = e.matmul(pa[0:1, 0:ncols], c.ones_bf[:, 0:1],
                         xbf[:, ch * ncols:(ch + 1) * ncols],
                         start=(ch == 0), stop=(ch == nch - 1))
        return i
    kb.step("tensor", s1, reads=[trk_x, c.trk(c.ones_bf)], writes=[c.trk(pa)])
    sqtok = []
    for ch in range(nch):
        s = sq_par[ch % 2]
        kb.step("scalar", lambda e, s=s, ch=ch: e.activation(
            s[:, 0:ncols], xbf[:, ch * ncols:(ch + 1) * ncols], AF.Square),
            reads=[trk_x], writes=[c.trk(c.sqs)])
        kb.step("tensor", lambda e, s=s, ch=ch: e.matmul(
            pb[0:1, 0:ncols], c.ones_bf[:, 0:1], s[:, 0:ncols],
            start=(ch == 0), stop=(ch == nch - 1)),
            reads=[c.trk(c.sqs)], writes=[c.trk(pb)])
    n = float(nch * 128)
    kb.step("scalar", lambda e: e.activation(c.rows_a[0:1, 0:ncols], pa[0:1, 0:ncols],
            AF.Copy, scale=1.0 / n), reads=[c.trk(pa)], writes=[c.trk(c.rows_a)])
    kb.step("scalar", lambda e: e.activation(c.rows_c[0:1, 0:ncols], c.rows_a[0:1, 0:ncols],
            AF.Square), reads=[c.trk(c.rows_a)], writes=[c.trk(c.rows_c)])
    kb.step("scalar", lambda e: e.activation(c.rows_b[0:1, 0:ncols], pb[0:1, 0:ncols],
            AF.Copy, scale=1.0 / n), reads=[c.trk(pb)], writes=[c.trk(c.rows_b)])
    kb.step("vector", lambda e: e.tensor_sub(c.rows_b[0:1, 0:ncols],
            c.rows_b[0:1, 0:ncols], c.rows_c[0:1, 0:ncols]),
            reads=[c.trk(c.rows_b), c.trk(c.rows_c)], writes=[c.trk(c.rows_b)])
    kb.step("scalar", lambda e: e.activation(c.rows_b[0:1, 0:ncols], c.rows_b[0:1, 0:ncols],
            AF.Ln, bias=c.epsc[0:1, 0:1]), reads=[c.trk(c.rows_b), c.trk(c.epsc)],
            writes=[c.trk(c.rows_b)])
    kb.step("scalar", lambda e: e.activation(c.rows_b[0:1, 0:ncols], c.rows_b[0:1, 0:ncols],
            AF.Exp, scale=-0.5), reads=[c.trk(c.rows_b)], writes=[c.trk(c.rows_b)])
    kb.step("scalar", lambda e: e.activation(c.rows_rstd[0:1, 0:ncols], c.rows_b[0:1, 0:ncols],
            AF.Copy), reads=[c.trk(c.rows_b)], writes=[c.trk(c.rows_rstd)])
    kb.step("vector", lambda e: e.tensor_mul(c.rows_mr[0:1, 0:ncols], c.rows_a[0:1, 0:ncols],
            c.rows_b[0:1, 0:ncols]), reads=[c.trk(c.rows_a), c.trk(c.rows_b)],
            writes=[c.trk(c.rows_mr)])
    bR, bM = c.ps_a, c.ps_b
    kb.step("tensor", lambda e: e.matmul(bR[:, 0:ncols], c.ones_bf[0:1, :],
            c.rows_rstd[0:1, 0:ncols], start=True, stop=True),
            reads=[c.trk(c.rows_rstd), c.trk(c.ones_bf)], writes=[c.trk(bR)])
    kb.step("tensor", lambda e: e.matmul(bM[:, 0:ncols], c.ones_bf[0:1, :],
            c.rows_mr[0:1, 0:ncols], start=True, stop=True),
            reads=[c.trk(c.rows_mr), c.trk(c.ones_bf)], writes=[c.trk(bM)])
    kb.step("scalar", lambda e: e.activation(c.bcR_sb[:, 0:ncols], bR[:, 0:ncols],
            AF.Copy), reads=[c.trk(bR)], writes=[c.trk(c.bcR_sb)])
    kb.step("scalar", lambda e: e.activation(c.bcM_sb[:, 0:ncols], bM[:, 0:ncols],
            AF.Copy), reads=[c.trk(bM)], writes=[c.trk(c.bcM_sb)])
    for ch in range(nch):
        dst = xbf[:, ch * ncols:(ch + 1) * ncols] if outs is None else outs[ch]
        to = trk_x if outs is None else trk_out
        kb.step("vector", lambda e, ch=ch, dst=dst: e.tensor_tensor(
            dst, xbf[:, ch * ncols:(ch + 1) * ncols],
            c.bcR_sb[:, 0:ncols], ALU.mult), reads=[trk_x, c.trk(c.bcR_sb)], writes=[to])
        kb.step("vector", lambda e, ch=ch, dst=dst: e.tensor_tensor(
            dst, dst, c.bcM_sb[:, 0:ncols], ALU.subtract),
            reads=[to, c.trk(c.bcM_sb)], writes=[to])


def emit_tail(core, ao, trk_ao, xf, trk_xf, src_x, trk_srcx, dst_x, trk_dstx,
              q, xfsem):
    kb, c = core.kb, core
    emit_ln(core, ao, trk_ao)
    for sub in range(2):
        cols = slice(q * QW + sub * 256, q * QW + sub * 256 + 256)
        srcv = src_x.rearrange("(c k) t -> k c t", k=128)[:, :, cols]
        dstv = dst_x.rearrange("(c k) t -> k c t", k=128)[:, :, cols]
        xfv = xf[:].rearrange("k (c t) -> k c t", c=CH)
        kb.step("sync", lambda e, d=xfv, s=srcv: e.dma_start(d, s),
                reads=[trk_srcx], writes=[trk_xf], sem=xfsem, amt=16)
        aov = ao[:].rearrange("k (c t) -> k c t", c=CH)[:, :, sub * 256:sub * 256 + 256]
        kb.step("vector", lambda e, xv=xf, av=aov: e.tensor_tensor(
            xv[:].rearrange("k (c t) -> k c t", c=CH), xv[:].rearrange("k (c t) -> k c t", c=CH),
            av, ALU.add), reads=[trk_xf, trk_ao], writes=[trk_xf],
            waits=[(xfsem, xfsem.n)])
        kb.step("sync", lambda e, d=dstv, s=xfv: e.dma_start(d, s),
                reads=[trk_xf], writes=[trk_dstx], sem=xfsem, amt=16)


class _TokTrk:
    def __init__(self, tok):
        self.tok = tok
    def rwait(self):
        return [self.tok]
    def wwait(self):
        return [self.tok]
    def setw(self, tok):
        pass
    def addr(self, tok):
        pass


def krtok_trk(tok):
    return _TokTrk(tok)


def merge_into(dst_trk, *srcs):
    for s in srcs:
        for sem, v in list(s.w.items()) + list(s.r.items()):
            dst_trk.r[sem] = max(dst_trk.r.get(sem, 0), v)


def emit_b0attn(core, seq, src_x, trk_srcx, dst_x, trk_dstx, st):
    kb, c = core.kb, core
    SB = c.CBASE
    S = seq
    kC = c.at(f"a0kC{S}", [128, CH * T], BF16, SB)
    Vtm = c.at(f"a0V{S}", [128, 16 * 1040], BF16, SB + 32768)
    kR = c.at(f"a0kR{S}", [128, T], BF16, SB + 66048)
    ckv = c.at(f"a0ckv{S}", [128, 2 * QW], BF16, SB + 70144)
    wdkv = c.at(f"a0wdkv{S}", [128, 2048], BF16, SB + 72192)
    wuk = c.at(f"a0wuk{S}", [128, 2048], BF16, SB + 76288)
    wuv = c.at(f"a0wuv{S}", [128, 2048], BF16, SB + 80384)
    wkr = c.at(f"a0wkr{S}", [128, 256], BF16, SB + 84480)
    skr = c.at(f"a0skr{S}", [32, QW], BF16, SB + 84992)
    sqR = c.at(f"a0sqR{S}", [128, QW], BF16, SB + 86016)
    sqRs = c.at(f"a0sqRs{S}", [32, QW], BF16, SB + 87040)
    xbf = [c.at(f"a0xbf{S}_{i}", [128, CH * QW], BF16, SB + 88064 + i * 8192)
           for i in range(2)]
    CQ0 = SB + 104448
    cQ = c.at(f"a0cQ{S}", [128, 32 * QW], BF16, CQ0)
    yfm = c.at(f"a0yfm{S}", [128, CH * QW], BF16, CQ0)
    ao = c.at(f"a0ao{S}", [128, CH * QW], BF16, CQ0 + 8192)
    xf = c.at(f"a0xf{S}", [128, CH * 256], F32, CQ0 + 16384)
    yn = c.at(f"a0yn{S}", [128, 1024], BF16, CQ0 + 24576)
    qC = c.at(f"a0qC{S}", [128, CH * QW], BF16, SB + 137216)
    qR = c.at(f"a0qR{S}", [128, 4 * QW], BF16, SB + 145408)
    qRs = c.at(f"a0qRs{S}", [32, 4 * QW], BF16, SB + 149504)
    probs = [c.at(f"a0pr{S}_{i}", [128, QW], BF16, SB + 153600 + i * 1024)
             for i in range(3)]
    probsT = [c.at(f"a0pT{S}_{i}", [128, QW], BF16, SB + 156672 + i * 1024)
              for i in range(3)]
    vm = [c.at(f"a0vm{S}_{i}", [128, QW], BF16, SB + 159744 + i * 1024)
          for i in range(2)]

    xlsem = [kb.sem(f"a0xl{S}_{i}") for i in range(2)]
    xfsem = kb.sem(f"a0xf{S}")
    kwsem = kb.sem(f"a0kw{S}")
    repsem = kb.sem(f"a0rep{S}")

    for (dst, key) in ((wdkv, "wdkv"), (wuk, "wuk"), (wuv, "wuv"), (wkr, "wkr")):
        src = c.Wsrc[key]
        nb, P, F = src.shape
        kb.step("gpsimd", lambda e, d=dst, s=src, nb=nb, F=F: e.dma_start(
            d[:, 0:nb * F].rearrange("p (g f) -> p g f", g=nb),
            s.rearrange("g p f -> p g f")),
            writes=[c.trk(dst)], sem=kwsem, amt=16)
    kwtok = (kwsem, kwsem.n)
    c.dump("wuk", wuk[:], [c.trk(wuk), krtok_trk(kwtok)])
    c.dump("rope4", c.rope4[:], [c.trk(c.rope4)])
    c.dump("wdkv", wdkv[:], [c.trk(wdkv), krtok_trk(kwtok)])
    c.dump("wkr", wkr[:], [c.trk(wkr), krtok_trk(kwtok)])

    rot = [c.ps_a, c.ps_b, c.ps_c]

    def mm_group(pbank, prow_sl, pcol_sl, tiles, rhss, reads, extra_waits=()):
        def f(e, tiles=tuple(tiles), rhss=tuple(rhss)):
            n = len(tiles)
            for i in range(n):
                ins = e.matmul(pbank[prow_sl, pcol_sl], tiles[i], rhss[i],
                               start=(i == 0), stop=(i == n - 1))
            return ins
        return kb.step("tensor", f, reads=list(reads), writes=[c.trk(pbank)],
                       waits=list(extra_waits))

    sqp = [c.sqs[:, 0:512], c.sqs[:, 512:1024]]

    def emit_A0(q):
        xb = xbf[q % 2]
        txb = c.trk(xb)
        srcv = src_x.rearrange("(c k) t -> k c t", k=128)[:, :, q * QW:(q + 1) * QW]
        kb.step("gpsimd", lambda e, d=xb, s=srcv: e.dma_start(
            d[:].rearrange("k (c t) -> k c t", c=CH), s),
            reads=[trk_srcx], writes=[txb], sem=xlsem[q % 2], amt=16)
        emit_ln(core, xb, txb)

    emit_A0(0)
    for q in range(NQ):
        qcols = slice(q * QW, (q + 1) * QW)
        xb = xbf[q % 2]
        txb = c.trk(xb)
        for m in range(2):
            pb = rot[m % 3]
            mm_group(pb, slice(None), slice(0, QW),
                     [wdkv[:, m * 1024 + cc * 128: m * 1024 + (cc + 1) * 128] for cc in range(CH)],
                     [xb[:, cc * QW:(cc + 1) * QW] for cc in range(CH)],
                     [txb, c.trk(wdkv)], extra_waits=[kwtok])
            kb.step("vector", lambda e, m=m, pb=pb: e.tensor_copy(
                ckv[:, m * QW:(m + 1) * QW], pb[:, 0:QW]),
                reads=[c.trk(pb)], writes=[c.trk(ckv)])
        for m in range(CH):
            pb = rot[m % 3]
            mm_group(pb, slice(None), slice(0, QW),
                     [wuk[:, m * 256 + cc * 128: m * 256 + (cc + 1) * 128] for cc in range(2)],
                     [ckv[:, cc * QW:(cc + 1) * QW] for cc in range(2)],
                     [c.trk(ckv), c.trk(wuk)])
            dst_kc = kC[:, m * T + q * QW: m * T + (q + 1) * QW]
            kb.step("vector", lambda e, d=dst_kc, pb=pb: e.tensor_copy(
                d, pb[:, 0:QW]), reads=[c.trk(pb)], writes=[c.trk(kC)])
        for m in range(CH):
            pb = rot[m % 3]
            vmb = vm[m % 2]
            mm_group(pb, slice(None), slice(0, QW),
                     [wuv[:, m * 256 + cc * 128: m * 256 + (cc + 1) * 128] for cc in range(2)],
                     [ckv[:, cc * QW:(cc + 1) * QW] for cc in range(2)],
                     [c.trk(ckv), c.trk(wuv)])
            kb.step("vector", lambda e, pb=pb, vmb=vmb: e.tensor_copy(
                vmb[:], pb[:, 0:QW]), reads=[c.trk(pb)], writes=[c.trk(vmb)])
            for tw in range(4):
                kc = 4 * q + tw
                ptb = [c.pt_a, c.pt_b][tw % 2]
                kb.step("tensor", lambda e, vmb=vmb, tw=tw, ptb=ptb: e.transpose(
                    ptb[:, 0:128], vmb[:, tw * 128:(tw + 1) * 128], c.ident[:]),
                    reads=[c.trk(vmb), c.trk(c.ident)], writes=[c.trk(ptb)])
                dstv = Vtm[:, kc * 1040 + 65 * (2 * m): kc * 1040 + 65 * (2 * m) + 130]
                kb.step("vector", lambda e, dstv=dstv, ptb=ptb: e.tensor_copy(
                    dstv.rearrange("p (h d) -> p h d", h=2)[:, :, 0:64],
                    ptb[:, 0:128].rearrange("p (h d) -> p h d", h=2)),
                    reads=[c.trk(ptb)], writes=[c.trk(Vtm)])
        for tw in range(4):
            kc = 4 * q + tw
            ones_ap = Vtm[:, kc * 1040: (kc + 1) * 1040].rearrange(
                "p (h d) -> p h d", d=65)[:, :, 64:65]
            kb.step("gpsimd", lambda e, a=ones_ap: e.memset(a, 1.0), writes=[c.trk(Vtm)])
        pb = rot[0]
        mm_group(pb, slice(0, 32), slice(0, QW),
                 [wkr[:, cc * 32:(cc + 1) * 32] for cc in range(CH)],
                 [xb[:, cc * QW:(cc + 1) * QW] for cc in range(CH)],
                 [txb, c.trk(wkr)], extra_waits=[kwtok])
        kr_dst = kR[0:32, qcols]
        kr_rope = c.rope4[0:32, qcols]
        kb.step("vector", lambda e, pb=pb, d=kr_dst, rp=kr_rope: e.tensor_tensor(
            d, pb[0:32, 0:QW], rp, ALU.mult),
            reads=[c.trk(pb), c.trk(c.rope4)], writes=[c.trk(kR)])
        for j in range(1, 4):
            kr_to = kR[32 * j:32 * (j + 1), qcols]
            kb.step("gpsimd", lambda e, d=kr_to, s=kr_dst: e.dma_start(d, s),
                reads=[c.trk(kR)], writes=[c.trk(kR)], sem=repsem, amt=16)
        krtok = (repsem, repsem.n)
        c.dump("ckv0", ckv[:], [c.trk(ckv)])
        c.dump("kC0", kC[:, 0:T], [c.trk(kC)])
        c.dump("kR0", kR[:, 0:QW], [c.trk(kR), krtok_trk(krtok)])
        c.dump("vtm0", Vtm[:, 0:2080], [c.trk(Vtm)])
        kb.step("scalar", lambda e, s=kr_dst: e.activation(skr[0:32, 0:QW], s,
                AF.Square), reads=[c.trk(kR)], waits=[krtok], writes=[c.trk(skr)])
        for m in range(CH):
            sq = sqp[m % 2]
            tsq = c.trk(c.sqs)
            src_kc = kC[:, m * T + q * QW: m * T + (q + 1) * QW]
            kb.step("scalar", lambda e, sq=sq, s=src_kc: e.activation(
                sq, s, AF.Square), reads=[c.trk(kC)], writes=[tsq])
            for hh in range(2):
                h = 2 * m + hh
                prow = rot[h % 3]
                def knorm(e, sq=sq, hh=hh, prow=prow):
                    e.matmul(prow[0:1, 0:QW], c.ones_bf[64 * hh:64 * hh + 64, 0:1],
                             sq[64 * hh:64 * hh + 64, :], start=True, stop=False)
                    return e.matmul(prow[0:1, 0:QW], c.ones_bf[0:32, 0:1],
                                    skr[0:32, 0:QW], start=False, stop=True)
                kb.step("tensor", knorm, reads=[tsq, c.trk(skr), c.trk(c.ones_bf)],
                        writes=[c.trk(prow)])
                if q == 0:
                    kb.step("vector", lambda e, h=h, prow=prow: e.reduce_max(
                        c.kmax[0:1, h:h + 1], prow[0:1, 0:QW], axis=AX.X),
                        reads=[c.trk(prow)], writes=[c.trk(c.kmax)])
                else:
                    kb.step("vector", lambda e, h=h, prow=prow: e.reduce_max(
                        c.kmax[0:1, 16 + h:17 + h], prow[0:1, 0:QW], axis=AX.X),
                        reads=[c.trk(prow)], writes=[c.trk(c.kmax)])
                    kb.step("vector", lambda e, h=h: e.tensor_max(
                        c.kmax[0:1, h:h + 1], c.kmax[0:1, h:h + 1],
                        c.kmax[0:1, 16 + h:17 + h]),
                        reads=[c.trk(c.kmax)], writes=[c.trk(c.kmax)])
        c.dump("kmax", c.kmax[:], [c.trk(c.kmax)])
        for bi in range(16):
            hld = st["wdq"].load(bi)
            for g in range(2):
                m = bi * 2 + g
                pb = rot[m % 3]
                mm_group(pb, slice(None), slice(0, QW),
                         [hld[0][:, g * 1024 + cc * 128: g * 1024 + (cc + 1) * 128]
                          for cc in range(CH)],
                         [xb[:, cc * QW:(cc + 1) * QW] for cc in range(CH)],
                         [txb, hld[1]], extra_waits=[hld[2]])
                kb.step("vector", lambda e, m=m, pb=pb: e.tensor_copy(
                    cQ[:, m * QW:(m + 1) * QW], pb[:, 0:QW]),
                    reads=[c.trk(pb)], writes=[c.trk(cQ)])
        for m in range(CH):
            hld = st["wuq"].load(m)
            pb = rot[m % 3]
            mm_group(pb, slice(None), slice(0, QW),
                     [hld[0][:, cc * 128:(cc + 1) * 128] for cc in range(32)],
                     [cQ[:, cc * QW:(cc + 1) * QW] for cc in range(32)],
                     [c.trk(cQ), hld[1]], extra_waits=[hld[2]])
            kb.step("vector", lambda e, m=m, pb=pb: e.tensor_copy(
                qC[:, m * QW:(m + 1) * QW], pb[:, 0:QW]),
                reads=[c.trk(pb)], writes=[c.trk(qC)])
        for m in range(4):
            hld = st["wqr"].load(m)
            pb = rot[m % 3]
            mm_group(pb, slice(None), slice(0, QW),
                     [hld[0][:, cc * 128:(cc + 1) * 128] for cc in range(32)],
                     [cQ[:, cc * QW:(cc + 1) * QW] for cc in range(32)],
                     [c.trk(cQ), hld[1]], extra_waits=[hld[2]])
            qr_rope = c.rope4[:, qcols]
            kb.step("vector", lambda e, m=m, pb=pb, rp=qr_rope: e.tensor_tensor(
                qR[:, m * QW:(m + 1) * QW], pb[:, 0:QW], rp, ALU.mult),
                reads=[c.trk(pb), c.trk(c.rope4)], writes=[c.trk(qR)])
            kb.step("gpsimd", lambda e, m=m: e.dma_start(
                qRs[0:32, m * QW:(m + 1) * QW], qR[96:128, m * QW:(m + 1) * QW]),
                reads=[c.trk(qR)], writes=[c.trk(qRs)], sem=repsem, amt=16)
        qstok = (repsem, repsem.n)
        c.dump("cQ0", cQ[:, 0:QW], [c.trk(cQ)])
        c.dump("qC0", qC[:], [c.trk(qC)])
        c.dump("qR0", qR[:], [c.trk(qR)])
        for ov in (yfm, ao, xf, yn):
            merge_into(c.trk(ov), c.trk(cQ))
        kb.step("scalar", lambda e: e.activation(c.kmaxbf[0:1, 0:16],
                c.kmax[0:1, 0:16], AF.Copy), reads=[c.trk(c.kmax)],
                writes=[c.trk(c.kmaxbf)])
        kb.step("tensor", lambda e: e.matmul(c.ps_c[:, 16:32], c.ones_bf[0:1, :],
                c.kmaxbf[0:1, 0:16], start=True, stop=True),
                reads=[c.trk(c.kmaxbf), c.trk(c.ones_bf)], writes=[c.trk(c.ps_c)])
        fl = c.pe_flush()
        kb.step("vector", lambda e: e.tensor_copy(c.kmaxsb[:], c.ps_c[:, 16:32]),
                reads=[c.trk(c.ps_c)], writes=[c.trk(c.kmaxsb)], waits=[fl])
        c.dump("kmaxsb", c.kmaxsb[:], [c.trk(c.kmaxsb)])
        pcnt = [0]
        for qq in range(4):
            i_qt = 4 * q + qq
            klen = (i_qt + 1) * 128
            first_nm = [True]
            for mc in range(CH):
                sq = sqp[mc % 2]
                kb.step("scalar", lambda e, mc=mc, sq=sq, qq=qq: e.activation(
                    sq[:, 0:128], qC[:, mc * QW + qq * 128: mc * QW + (qq + 1) * 128],
                    AF.Square), reads=[c.trk(qC)], writes=[c.trk(c.sqs)])
                kb.step("tensor", lambda e, mc=mc, sq=sq, f=first_nm[0]: e.matmul(
                    c.ps_c[:, 2 * mc:2 * mc + 2], sq[:, 0:128], c.sel64[:],
                    start=f, stop=False, skip_group_check=True),
                    reads=[c.trk(c.sqs), c.trk(c.sel64)], writes=[c.trk(c.ps_c)])
                first_nm[0] = False
            for mq in range(4):
                sq = sqp[mq % 2]
                kb.step("scalar", lambda e, mq=mq, sq=sq, qq=qq: e.activation(
                    sq[:, 0:128], qR[:, mq * QW + qq * 128: mq * QW + (qq + 1) * 128],
                    AF.Square), reads=[c.trk(qR)], writes=[c.trk(c.sqs)])
                kb.step("tensor", lambda e, mq=mq, sq=sq, last=(mq == 3): e.matmul(
                    c.ps_c[:, 4 * mq:4 * mq + 4], sq[:, 0:128], c.sel32[:],
                    start=False, stop=last, skip_group_check=True),
                    reads=[c.trk(c.sqs), c.trk(c.sel32)], writes=[c.trk(c.ps_c)])
            fl2 = c.pe_flush()
            kb.step("vector", lambda e: e.tensor_copy(c.bias3[:, 0:16], c.ps_c[:, 0:16]),
                    reads=[c.trk(c.ps_c)], writes=[c.trk(c.bias3)], waits=[fl2])
            kb.step("vector", lambda e: e.tensor_tensor(c.bias3[:, 16:32],
                    c.bias3[:, 0:16], c.kmaxsb[:], ALU.mult),
                    reads=[c.trk(c.bias3), c.trk(c.kmaxsb)], writes=[c.trk(c.bias3)])
            kb.step("scalar", lambda e: e.activation(c.bias3[:, 32:48],
                    c.bias3[:, 16:32], AF.Sqrt), reads=[c.trk(c.bias3)],
                    writes=[c.trk(c.bias3)])
            kb.step("vector", lambda e: e.tensor_scalar(c.bias3[:, 32:48],
                    c.bias3[:, 32:48], -SCALE0, None, ALU.mult),
                    reads=[c.trk(c.bias3)], writes=[c.trk(c.bias3)])
            c.dump("bias0", c.bias3[:], [c.trk(c.bias3)])
            nw = ceil_div(klen, QW)
            units = [(h, w) for h in range(H0) for w in range(nw)]

            def emit_sc(h, w):
                mc, hc, j, mq = h // 2, h % 2, h % 4, h // 4
                L = min(QW, klen - w * QW)
                par = pcnt[0] % 3
                pcnt[0] += 1
                sc = [c.ps_a, c.ps_b, c.ps_c][par]
                pr = probs[par]
                qC_l = qC[:, mc * QW + qq * 128: mc * QW + (qq + 1) * 128][
                    64 * hc:64 * hc + 64, :]
                if j < 3:
                    qR_l = qR[:, mq * QW + qq * 128: mq * QW + (qq + 1) * 128][
                        32 * j:32 * j + 32, :]
                    kR_l = kR[32 * j:32 * j + 32, w * QW:w * QW + L]
                else:
                    qR_l = qRs[0:32, mq * QW + qq * 128: mq * QW + (qq + 1) * 128]
                    kR_l = kR[0:32, w * QW:w * QW + L]
                kC_l = kC[:, mc * T + w * QW: mc * T + w * QW + L][
                    64 * hc:64 * hc + 64, :]
                def scmm(e, sc=sc, qC_l=qC_l, kC_l=kC_l, qR_l=qR_l, kR_l=kR_l, L=L):
                    e.matmul(sc[:, 0:L], qC_l, kC_l, start=True, stop=False)
                    e.matmul(sc[:, 0:L], qR_l, kR_l, start=False, stop=True)
                    return e.matmul(sc[0:1, 0:2], c.ident[0:1, 1:2],
                                    c.ones_bf[0:1, 0:2], start=False, stop=True,
                                    skip_group_check=True)
                kb.step("tensor", scmm,
                        reads=[c.trk(qC), c.trk(qR), c.trk(qRs), c.trk(kC), c.trk(kR)],
                        writes=[c.trk(sc)])
                kb.step("scalar", lambda e, sc=sc, pr=pr, h=h, L=L: e.activation(
                    pr[:, 0:L], sc[:, 0:L], AF.Exp,
                    bias=c.bias3[:, 32 + h:33 + h], scale=SCALE0),
                    reads=[c.trk(sc), c.trk(c.bias3)], writes=[c.trk(pr)])
                if w == nw - 1:
                    kb.step("gpsimd", lambda e, pr=pr, L=L: e.affine_select(
                        out=pr[:, L - 128:L], in_=pr[:, L - 128:L],
                        compare_op=ALU.is_ge, fill=0.0, base=0,
                        pattern=[[-1, 128]], channel_multiplier=1),
                        reads=[c.trk(pr)], writes=[c.trk(pr)])
                return par

            def emit_tr(h, w, par):
                L = min(QW, klen - w * QW)
                pr = probs[par]
                pT = probsT[par]
                ptp = [c.pt_a, c.pt_b][par % 2]
                nch = ceil_div(L, 128)
                def trgrp(e, pr=pr, ptp=ptp, nch=nch):
                    for tch in range(nch):
                        ins = e.transpose(ptp[:, tch * 128:(tch + 1) * 128],
                                          pr[:, tch * 128:(tch + 1) * 128], c.ident[:])
                    return ins
                kb.step("tensor", trgrp, reads=[c.trk(pr), c.trk(c.ident)],
                        writes=[c.trk(ptp)])
                kb.step("vector", lambda e, pT=pT, ptp=ptp, L=L: e.tensor_copy(
                    pT[:, 0:L], ptp[:, 0:L]), reads=[c.trk(ptp)], writes=[c.trk(pT)])

            def emit_pv(h, w, par):
                L = min(QW, klen - w * QW)
                pT = probsT[par]
                nch = ceil_div(L, 128)
                yc = ycol0(h)
                def pvgrp(e, pT=pT, h=h, w=w, nch=nch, yc=yc,
                          first=(w == 0), last=(w == nw - 1)):
                    for tch in range(nch):
                        kc = 4 * w + tch
                        ins = e.matmul(c.ps_y[:, yc:yc + 65],
                                       pT[:, tch * 128:(tch + 1) * 128],
                                       Vtm[:, kc * 1040 + 65 * h: kc * 1040 + 65 * (h + 1)],
                                       start=(first and tch == 0),
                                       stop=(last and tch == nch - 1))
                    return ins
                kb.step("tensor", pvgrp, reads=[c.trk(pT), c.trk(Vtm)],
                        writes=[c.trk(c.ps_y)])

            plist = []
            for u in units:
                par = emit_sc(*u)
                plist.append((u[0], u[1], par))
                if len(plist) >= 2:
                    emit_tr(*plist[-2])
                if len(plist) >= 3:
                    emit_pv(*plist[-3])
            emit_tr(*plist[-1])
            if len(plist) >= 2:
                emit_pv(*plist[-2])
            emit_pv(*plist[-1])
            fl3 = c.pe_flush()
            for g, ng in ((0, 7), (1, 7), (2, 2)):
                kb.step("vector", lambda e, g=g, ng=ng: e.reciprocal(
                    c.recips[:, 7 * g:7 * g + ng],
                    bass.AP(c.ps_y, 512 * g + 64, [[1536, 128], [65, ng]])),
                    reads=[c.trk(c.ps_y)], writes=[c.trk(c.recips)], waits=[fl3])
            for h in range(H0):
                yc = ycol0(h)
                kb.step("scalar", lambda e, h=h, yc=yc: e.mul(
                    yn[:, 64 * h:64 * (h + 1)], c.ps_y[:, yc:yc + 64],
                    c.recips[:, h:h + 1]),
                    reads=[c.trk(c.ps_y), c.trk(c.recips)], writes=[c.trk(yn)])
            c.dump("pr0", probs[0][:], [c.trk(probs[0])])
            c.dump("yn0", yn[:], [c.trk(yn)])
            c.dump("recips0", c.recips[:], [c.trk(c.recips)])
            for half in range(2):
                ptp = [c.pt_a, c.pt_b][half]
                def ytr(e, half=half, ptp=ptp):
                    for c2 in range(4):
                        ins = e.transpose(ptp[:, c2 * 128:(c2 + 1) * 128],
                                          yn[:, (half * 4 + c2) * 128:(half * 4 + c2 + 1) * 128],
                                          c.ident[:])
                    return ins
                kb.step("tensor", ytr, reads=[c.trk(yn), c.trk(c.ident)], writes=[c.trk(ptp)])
                dstv = yfm[:].rearrange("p (c t) -> p c t", c=CH)[
                    :, half * 4:(half + 1) * 4, qq * 128:(qq + 1) * 128]
                kb.step("vector", lambda e, ptp=ptp, dstv=dstv: e.tensor_copy(
                    dstv, ptp[:].rearrange("p (c t) -> p c t", c=4)),
                    reads=[c.trk(ptp)], writes=[c.trk(yfm)])
        c.dump("yfm0", yfm[:], [c.trk(yfm)])
        if q + 1 < NQ:
            emit_A0(q + 1)
        for bi in range(2):
            hld = st["wo0"].load(bi)
            for g in range(4):
                m = bi * 4 + g
                pb = rot[m % 3]
                mm_group(pb, slice(None), slice(0, QW),
                         [hld[0][:, g * 1024 + cc * 128: g * 1024 + (cc + 1) * 128]
                          for cc in range(CH)],
                         [yfm[:, cc * QW:(cc + 1) * QW] for cc in range(CH)],
                         [c.trk(yfm), hld[1]], extra_waits=[hld[2]])
                kb.step("vector", lambda e, m=m, pb=pb: e.tensor_copy(
                    ao[:, m * QW:(m + 1) * QW], pb[:, 0:QW]),
                    reads=[c.trk(pb)], writes=[c.trk(ao)])
        c.dump("ao0", ao[:], [c.trk(ao)])
        emit_tail(core, ao, c.trk(ao), xf, c.trk(xf), src_x, trk_srcx,
                  dst_x, trk_dstx, q, xfsem)
        merge_into(c.trk(cQ), c.trk(yfm), c.trk(ao), c.trk(xf), c.trk(yn))


WSPECS = [
    ("wdq", (32, 128, 1024), F32), ("wuq", (8, 128, 4096), F32),
    ("wqr", (4, 128, 4096), F32), ("wdkv", (2, 128, 1024), F32),
    ("wuk", (8, 128, 256), F32), ("wuv", (8, 128, 256), F32),
    ("wkr", (1, 128, 256), F32), ("wo0", (8, 128, 1024), F32),
    ("wg", (24, 128, 1024), I32), ("wu", (24, 128, 1024), I32),
    ("wd", (8, 128, 3072), I32),
    ("wq1", (8, 128, 1024), F32), ("wkd", (1, 128, 512), F32),
    ("wvd", (1, 128, 512), F32), ("wku", (1, 32, 128), F32),
    ("wvu", (1, 32, 128), F32), ("wo1", (8, 128, 1024), F32),
    ("kern", (8, 128, 1024), F32), ("bias1", (128, 8), F32),
    ("la", (1, 128, 128), F32), ("lb", (8, 16, 128), F32),
]


def build_program(stages=("b0attn", "b0ffn", "b1attn", "b1ffn"), dbg=False):
    nc = bass.Bass("TRN2", target_bir_lowering=False)
    kb = Kb(nc)
    core = Core(nc, kb)
    core.kb = kb
    core.dbg = dbg
    kb.global_waits = []
    xin = nc.dram_tensor("x", [NSEQ, 1024, T], F32, kind="ExternalInput").ap()
    out = nc.dram_tensor("out", [NSEQ, 1024, T], F32, kind="ExternalOutput").ap()
    xs0 = nc.dram_tensor("xs0", [1024, T], F32, kind="Internal").ap()
    xs1 = nc.dram_tensor("xs1", [1024, T], F32, kind="Internal").ap()
    core.Wsrc = {}
    for name, shape, dt in WSPECS:
        core.Wsrc[name] = nc.dram_tensor(name, list(shape), dt, kind="ExternalInput").ap()
    t_xin = Trk(); t_out = Trk(); t_xs0 = Trk(); t_xs1 = Trk()

    setup_consts(core)

    WB0 = core.CBASE + 161792
    st = {}
    st["wdq"] = WStream(kb, "wdq", core.Wsrc["wdq"], 32, 128, 1024,
                        core.MBASE + WB0, batch=2)
    st["wuq"] = WStream(kb, "wuq", core.Wsrc["wuq"], 8, 128, 4096,
                        core.MBASE + WB0 + 8192, batch=1)
    st["wqr"] = WStream(kb, "wqr", core.Wsrc["wqr"], 4, 128, 4096,
                        None, batch=1, share=st["wuq"])
    st["wo0"] = WStream(kb, "wo0", core.Wsrc["wo0"], 8, 128, 1024,
                        None, batch=4, share=st["wuq"])
    FWB = core.MBASE + core.CBASE + 147456
    st["wg"] = WStream(kb, "wg", core.Wsrc["wg"], 24, 128, 1024, FWB, batch=2)
    st["wu"] = WStream(kb, "wu", core.Wsrc["wu"], 24, 128, 1024, FWB + 8192, batch=2)
    st["wd"] = WStream(kb, "wd", core.Wsrc["wd"], 8, 128, 3072, FWB + 16384, batch=1)

    emitters = {"b0attn": emit_b0attn, "b0ffn": emit_b0ffn,
                "b1attn": emit_b1attn, "b1ffn": emit_b1ffn}
    all_stages = ["b0attn", "b0ffn", "b1attn", "b1ffn"]
    active = [s for s in all_stages if s in stages]
    for seq in range(NSEQ):
        src, tsrc = xin[seq], t_xin
        scratch = [(xs0, t_xs0), (xs1, t_xs1)]
        for k, sname in enumerate(active):
            if k == len(active) - 1:
                dst, tdst = out[seq], t_out
            else:
                dst, tdst = scratch[k % 2]
            kb.global_waits = kb.barrier_token() + [kb.rope_tok]
            emitters[sname](core, seq, src, tsrc, dst, tdst, st)
            src, tsrc = dst, tdst
    dummy = core.at("fin_dummy", [1, 4], F32, core.CBASE + 184320 - 64)
    kb.step("gpsimd", lambda e: e.memset(dummy[:], 0.0), reads=[t_out],
            writes=[core.trk(dummy)], waits=list(getattr(kb, "dbg_sems", [])))
    with nc.Block() as block:
        kb.finalize(block)
    return nc


def kernel(**inputs):
    Wp, x_fm = prep_host(inputs)
    nc = build_program()
    in_maps = []
    for cid in range(8):
        m = {"x": np.ascontiguousarray(x_fm[2 * cid:2 * cid + 2])}
        m.update(Wp)
        in_maps.append(m)
    res = run_bass_kernel_spmd(nc, in_maps, core_ids=list(range(8)))
    y = np.concatenate([r["out"] for r in res.results], 0)
    return np.ascontiguousarray(y.transpose(0, 2, 1))


def emit_b0ffn(core, seq, src_x, trk_srcx, dst_x, trk_dstx, st):
    kb, c = core.kb, core
    SB = c.CBASE
    S = seq
    xn = c.at(f"f0xn{S}", [128, CH * T], BF16, SB)
    hf = c.at(f"f0h{S}", [128, FFC * T], BF16, SB + 32768)
    xbf = [c.at(f"f0xbf{S}_{i}", [128, CH * QW], BF16, SB + 131072 + i * 8192)
           for i in range(2)]
    out_q = c.at(f"f0out{S}", [128, CH * QW], BF16, SB + 131072)
    xf = c.at(f"f0xf{S}", [128, CH * 256], F32, SB + 139264)
    WB = SB + 147456
    xlsem = [kb.sem(f"f0xl{S}_{i}") for i in range(2)]
    xfsem = kb.sem(f"f0xf{S}")

    sqp = [c.sqs[:, 0:512], c.sqs[:, 512:1024]]
    rot4 = [c.ps_a, c.ps_b, c.ps_c, None]

    for q in range(NQ):
        xb = xbf[q % 2]
        txb = c.trk(xb)
        srcv = src_x.rearrange("(c k) t -> k c t", k=128)[:, :, q * QW:(q + 1) * QW]
        kb.step("gpsimd", lambda e, d=xb, s=srcv: e.dma_start(
            d[:].rearrange("k (c t) -> k c t", c=CH), s),
            reads=[trk_srcx], writes=[txb], sem=xlsem[q % 2], amt=16)
        outs = [xn[:, ch * T + q * QW: ch * T + (q + 1) * QW] for ch in range(CH)]
        emit_ln(core, xb, txb, outs=outs, trk_out=c.trk(xn))
    merge_into(c.trk(out_q), c.trk(xbf[0]))
    merge_into(c.trk(xf), c.trk(xbf[1]))
    c.dump("f_xn", xn[:, 0:T], [c.trk(xn)])

    for bi in range(12):
        hg = st["wg"].load(bi)
        hu = st["wu"].load(bi)
        for g in range(2):
            m = bi * 2 + g
            for w in range(NQ):
                pg = rot4[(2 * w) % 3]
                pu = rot4[(2 * w + 1) % 3]
                sg = sqp[w % 2]
                def gmm(e, hg=hg, g=g, w=w, pg=pg):
                    for cc in range(CH):
                        ins = e.matmul(pg[:, 0:QW],
                                       hg[0][:, g * 1024 + cc * 128:g * 1024 + (cc + 1) * 128],
                                       xn[:, cc * T + w * QW: cc * T + (w + 1) * QW],
                                       start=(cc == 0), stop=(cc == CH - 1))
                    return ins
                kb.step("tensor", gmm, reads=[c.trk(xn), hg[1]], writes=[c.trk(pg)],
                        waits=[hg[2]])
                def umm(e, hu=hu, g=g, w=w, pu=pu):
                    for cc in range(CH):
                        ins = e.matmul(pu[:, 0:QW],
                                       hu[0][:, g * 1024 + cc * 128:g * 1024 + (cc + 1) * 128],
                                       xn[:, cc * T + w * QW: cc * T + (w + 1) * QW],
                                       start=(cc == 0), stop=(cc == CH - 1))
                    return ins
                kb.step("tensor", umm, reads=[c.trk(xn), hu[1]], writes=[c.trk(pu)],
                        waits=[hu[2]])
                kb.step("scalar", lambda e, pg=pg, sg=sg: e.activation(
                    sg, pg[:, 0:QW], AF.Silu, scale=1.0 / 127.0),
                    reads=[c.trk(pg)], writes=[c.trk(c.sqs)])
                hdst = hf[:, m * T + w * QW: m * T + (w + 1) * QW]
                kb.step("vector", lambda e, hdst=hdst, sg=sg, pu=pu: e.tensor_tensor(
                    hdst, sg, pu[:, 0:QW], ALU.mult),
                    reads=[c.trk(c.sqs), c.trk(pu)], writes=[c.trk(hf)])

    c.dump("f_h", hf[:, 0:T], [c.trk(hf)])
    for q in range(NQ):
        for m2 in range(CH):
            hw = st["wd"].load(m2)
            pb = rot4[m2 % 3]
            def dmm(e, hw=hw, pb=pb, q=q):
                for kc in range(FFC):
                    ins = e.matmul(pb[:, 0:QW], hw[0][:, kc * 128:(kc + 1) * 128],
                                   hf[:, kc * T + q * QW: kc * T + (q + 1) * QW],
                                   start=(kc == 0), stop=(kc == FFC - 1))
                return ins
            kb.step("tensor", dmm, reads=[c.trk(hf), hw[1]], writes=[c.trk(pb)],
                    waits=[hw[2]])
            kb.step("scalar", lambda e, m2=m2, pb=pb: e.activation(
                out_q[:, m2 * QW:(m2 + 1) * QW], pb[:, 0:QW], AF.Copy,
                scale=1.0 / (127.0 * 127.0)),
                reads=[c.trk(pb)], writes=[c.trk(out_q)])
        c.dump("f_out", out_q[:], [c.trk(out_q)])
        emit_tail(core, out_q, c.trk(out_q), xf, c.trk(xf), src_x, trk_srcx,
                  dst_x, trk_dstx, q, xfsem)


def ycol1(h):
    return (h // 3) * 512 + (h % 3) * 129


def emit_b1attn(core, seq, src_x, trk_srcx, dst_x, trk_dstx, st):
    kb, c = core.kb, core
    SB = c.CBASE
    S = seq
    wq = c.at(f"a1wq{S}", [128, CH * 1024], BF16, SB)
    wo = c.at(f"a1wo{S}", [128, CH * 1024], BF16, SB + 16384)
    wkd = c.at(f"a1wkd{S}", [128, 512], BF16, SB + 32768)
    wvd = c.at(f"a1wvd{S}", [128, 512], BF16, SB + 33792)
    wku = c.at(f"a1wku{S}", [64, 128], BF16, SB + 34816)
    wvu = c.at(f"a1wvu{S}", [64, 128], BF16, SB + 35072)
    kf = c.at(f"a1kf{S}", [128, KV1 * T], BF16, SB + 35328)
    Vtm = c.at(f"a1V{S}", [128, 16 * 258], BF16, SB + 43520)
    klat = c.at(f"a1klat{S}", [64, QW], BF16, SB + 51776)
    vlat = c.at(f"a1vlat{S}", [64, QW], BF16, SB + 52800)
    xbf = [c.at(f"a1xbf{S}_{i}", [128, CH * QW], BF16, SB + 53824 + i * 8192)
           for i in range(2)]
    qf = c.at(f"a1qf{S}", [128, CH * QW], BF16, SB + 70208)
    yfm = c.at(f"a1yfm{S}", [128, CH * QW], BF16, SB + 78400)
    ao = c.at(f"a1ao{S}", [128, CH * QW], BF16, SB + 86592)
    xf = c.at(f"a1xf{S}", [128, CH * 256], F32, SB + 94784)
    yn = c.at(f"a1yn{S}", [128, 1024], BF16, SB + 102976)
    probs = [c.at(f"a1pr{S}_{i}", [128, QW], BF16, SB + 105024 + i * 1024)
             for i in range(3)]
    probsT = [c.at(f"a1pT{S}_{i}", [128, QW], BF16, SB + 108096 + i * 1024)
              for i in range(3)]
    vm = [c.at(f"a1vm{S}_{i}", [128, QW], BF16, SB + 111168 + i * 1024)
          for i in range(2)]

    xlsem = [kb.sem(f"a1xl{S}_{i}") for i in range(2)]
    xfsem = kb.sem(f"a1xf{S}")
    kwsem = kb.sem(f"a1kw{S}")

    for (dst, key) in ((wq, "wq1"), (wo, "wo1"), (wkd, "wkd"), (wvd, "wvd")):
        srcw = c.Wsrc[key]
        nb, P, F = srcw.shape
        kb.step("gpsimd", lambda e, d=dst, s=srcw, nb=nb, F=F: e.dma_start(
            d[:, 0:nb * F].rearrange("p (g f) -> p g f", g=nb),
            s.rearrange("g p f -> p g f")),
            writes=[c.trk(dst)], sem=kwsem, amt=16)
    for (dst, key) in ((wku, "wku"), (wvu, "wvu")):
        srcw = c.Wsrc[key]
        for rr in range(2):
            kb.step("gpsimd", lambda e, d=dst, s=srcw, rr=rr: e.dma_start(
                d[32 * rr:32 * rr + 32, :], s[0]),
                writes=[c.trk(dst)], sem=kwsem, amt=16)
    kwtok = (kwsem, kwsem.n)

    rot = [c.ps_a, c.ps_b, c.ps_c]
    sqp = [c.sqs[:, 0:512], c.sqs[:, 512:1024]]

    def mm_group(pbank, prow_sl, pcol_sl, tiles, rhss, reads, extra_waits=()):
        def f(e, tiles=tuple(tiles), rhss=tuple(rhss)):
            n = len(tiles)
            for i in range(n):
                ins = e.matmul(pbank[prow_sl, pcol_sl], tiles[i], rhss[i],
                               start=(i == 0), stop=(i == n - 1))
            return ins
        return kb.step("tensor", f, reads=list(reads), writes=[c.trk(pbank)],
                       waits=list(extra_waits))

    def emit_A1(q):
        xb = xbf[q % 2]
        txb = c.trk(xb)
        srcv = src_x.rearrange("(c k) t -> k c t", k=128)[:, :, q * QW:(q + 1) * QW]
        kb.step("gpsimd", lambda e, d=xb, s=srcv: e.dma_start(
            d[:].rearrange("k (c t) -> k c t", c=CH), s),
            reads=[trk_srcx], writes=[txb], sem=xlsem[q % 2], amt=16)
        emit_ln(core, xb, txb)

    emit_A1(0)
    for q in range(NQ):
        qcols = slice(q * QW, (q + 1) * QW)
        xb = xbf[q % 2]
        txb = c.trk(xb)
        for (lat, wdn) in ((klat, wkd), (vlat, wvd)):
            pb = rot[0] if lat is klat else rot[1]
            mm_group(pb, slice(0, 64), slice(0, QW),
                     [wdn[:, cc * 64:(cc + 1) * 64] for cc in range(CH)],
                     [xb[:, cc * QW:(cc + 1) * QW] for cc in range(CH)],
                     [txb, c.trk(wdn)], extra_waits=[kwtok])
            kb.step("vector", lambda e, lat=lat, pb=pb: e.tensor_copy(
                lat[:], pb[0:64, 0:QW]), reads=[c.trk(pb)], writes=[c.trk(lat)])
        for kv in range(KV1):
            pb = rot[2]
            mm_group(pb, slice(None), slice(0, QW),
                     [wku[32 * kv:32 * kv + 32, :]],
                     [klat[32 * kv:32 * kv + 32, :]],
                     [c.trk(klat), c.trk(wku)], extra_waits=[kwtok])
            dstk = kf[:, kv * T + q * QW: kv * T + (q + 1) * QW]
            kb.step("vector", lambda e, d=dstk, pb=pb: e.tensor_copy(
                d, pb[:, 0:QW]), reads=[c.trk(pb)], writes=[c.trk(kf)])
            pb2 = rot[(kv + 1) % 3]
            mm_group(pb2, slice(None), slice(0, QW),
                     [wvu[32 * kv:32 * kv + 32, :]],
                     [vlat[32 * kv:32 * kv + 32, :]],
                     [c.trk(vlat), c.trk(wvu)], extra_waits=[kwtok])
            vmb = vm[kv % 2]
            kb.step("vector", lambda e, vmb=vmb, pb2=pb2: e.tensor_copy(
                vmb[:], pb2[:, 0:QW]), reads=[c.trk(pb2)], writes=[c.trk(vmb)])
            for tw in range(4):
                kc = 4 * q + tw
                ptb = [c.pt_a, c.pt_b][tw % 2]
                kb.step("tensor", lambda e, vmb=vmb, tw=tw, ptb=ptb: e.transpose(
                    ptb[:, 0:128], vmb[:, tw * 128:(tw + 1) * 128], c.ident[:]),
                    reads=[c.trk(vmb), c.trk(c.ident)], writes=[c.trk(ptb)])
                dstv = Vtm[:, kc * 258 + 129 * kv: kc * 258 + 129 * kv + 128]
                kb.step("vector", lambda e, dstv=dstv, ptb=ptb: e.tensor_copy(
                    dstv, ptb[:, 0:128]), reads=[c.trk(ptb)], writes=[c.trk(Vtm)])
        for tw in range(4):
            kc = 4 * q + tw
            ones_ap = Vtm[:, kc * 258:(kc + 1) * 258].rearrange(
                "p (h d) -> p h d", d=129)[:, :, 128:129]
            kb.step("gpsimd", lambda e, a=ones_ap: e.memset(a, 1.0), writes=[c.trk(Vtm)])
        for kv in range(KV1):
            sq = sqp[kv % 2]
            srck = kf[:, kv * T + q * QW: kv * T + (q + 1) * QW]
            kb.step("scalar", lambda e, sq=sq, s=srck: e.activation(
                sq, s, AF.Square), reads=[c.trk(kf)], writes=[c.trk(c.sqs)])
            prow = rot[kv % 3]
            def knorm(e, sq=sq, prow=prow):
                e.matmul(prow[0:1, 0:QW], c.ones_bf[:, 0:1], sq,
                         start=True, stop=True)
                return e.matmul(prow[0:1, 0:2], c.ident[0:1, 1:2],
                                c.ones_bf[0:1, 0:2], start=False, stop=True,
                                skip_group_check=True)
            kb.step("tensor", knorm, reads=[c.trk(c.sqs), c.trk(c.ones_bf),
                    c.trk(c.ident)], writes=[c.trk(prow)])
            if q == 0:
                kb.step("vector", lambda e, kv=kv, prow=prow: e.reduce_max(
                    c.kmax[0:1, kv:kv + 1], prow[0:1, 0:QW], axis=AX.X),
                    reads=[c.trk(prow)], writes=[c.trk(c.kmax)])
            else:
                kb.step("vector", lambda e, kv=kv, prow=prow: e.reduce_max(
                    c.kmax[0:1, 16 + kv:17 + kv], prow[0:1, 0:QW], axis=AX.X),
                    reads=[c.trk(prow)], writes=[c.trk(c.kmax)])
                kb.step("vector", lambda e, kv=kv: e.tensor_max(
                    c.kmax[0:1, kv:kv + 1], c.kmax[0:1, kv:kv + 1],
                    c.kmax[0:1, 16 + kv:17 + kv]),
                    reads=[c.trk(c.kmax)], writes=[c.trk(c.kmax)])
        kb.step("scalar", lambda e: e.activation(c.kmaxbf[0:1, 0:2],
                c.kmax[0:1, 0:2], AF.Copy), reads=[c.trk(c.kmax)],
                writes=[c.trk(c.kmaxbf)])
        kb.step("tensor", lambda e: e.matmul(c.ps_c[:, 16:18], c.ones_bf[0:1, :],
                c.kmaxbf[0:1, 0:2], start=True, stop=True),
                reads=[c.trk(c.kmaxbf), c.trk(c.ones_bf)], writes=[c.trk(c.ps_c)])
        fl = c.pe_flush()
        kb.step("vector", lambda e: e.tensor_copy(c.kmaxsb[:, 0:2], c.ps_c[:, 16:18]),
                reads=[c.trk(c.ps_c)], writes=[c.trk(c.kmaxsb)], waits=[fl])
        for m in range(CH):
            pb = rot[m % 3]
            mm_group(pb, slice(None), slice(0, QW),
                     [wq[:, m * 1024 + cc * 128: m * 1024 + (cc + 1) * 128]
                      for cc in range(CH)],
                     [xb[:, cc * QW:(cc + 1) * QW] for cc in range(CH)],
                     [txb, c.trk(wq)], extra_waits=[kwtok])
            kb.step("vector", lambda e, m=m, pb=pb: e.tensor_copy(
                qf[:, m * QW:(m + 1) * QW], pb[:, 0:QW]),
                reads=[c.trk(pb)], writes=[c.trk(qf)])
        pcnt = [0]
        for qq in range(4):
            i_qt = 4 * q + qq
            klen = (i_qt + 1) * 128
            first_nm = [True]
            for h in range(H1):
                sq = sqp[h % 2]
                srcq = qf[:, h * QW + qq * 128: h * QW + (qq + 1) * 128]
                kb.step("scalar", lambda e, sq=sq, s=srcq: e.activation(
                    sq[:, 0:128], s, AF.Square), reads=[c.trk(qf)],
                    writes=[c.trk(c.sqs)])
                kb.step("tensor", lambda e, h=h, sq=sq, f=first_nm[0]: e.matmul(
                    c.ps_c[:, h:h + 1], sq[:, 0:128], c.ones_bf[:, 0:1],
                    start=f, stop=(h == H1 - 1), skip_group_check=True),
                    reads=[c.trk(c.sqs), c.trk(c.ones_bf)], writes=[c.trk(c.ps_c)])
                first_nm[0] = False
            fl2 = c.pe_flush()
            kb.step("vector", lambda e: e.tensor_copy(c.bias3[:, 0:8], c.ps_c[:, 0:8]),
                    reads=[c.trk(c.ps_c)], writes=[c.trk(c.bias3)], waits=[fl2])
            for kv in range(KV1):
                kb.step("vector", lambda e, kv=kv: e.tensor_scalar(
                    c.bias3[:, 16 + 4 * kv:20 + 4 * kv], c.bias3[:, 4 * kv:4 * kv + 4],
                    c.kmaxsb[:, kv:kv + 1], None, ALU.mult),
                    reads=[c.trk(c.bias3), c.trk(c.kmaxsb)], writes=[c.trk(c.bias3)])
            kb.step("scalar", lambda e: e.activation(c.bias3[:, 32:40],
                    c.bias3[:, 16:24], AF.Sqrt), reads=[c.trk(c.bias3)],
                    writes=[c.trk(c.bias3)])
            kb.step("vector", lambda e: e.tensor_scalar(c.bias3[:, 32:40],
                    c.bias3[:, 32:40], -SCALE1, None, ALU.mult),
                    reads=[c.trk(c.bias3)], writes=[c.trk(c.bias3)])
            nw = ceil_div(klen, QW)
            units = [(h, w) for h in range(H1) for w in range(nw)]

            def emit_sc(h, w):
                L = min(QW, klen - w * QW)
                kv = h // 4
                par = pcnt[0] % 3
                pcnt[0] += 1
                sc = [c.ps_a, c.ps_b, c.ps_c][par]
                pr = probs[par]
                q_l = qf[:, h * QW + qq * 128: h * QW + (qq + 1) * 128]
                k_l = kf[:, kv * T + w * QW: kv * T + w * QW + L]
                def scmm(e, sc=sc, q_l=q_l, k_l=k_l, L=L):
                    e.matmul(sc[:, 0:L], q_l, k_l, start=True, stop=True)
                    return e.matmul(sc[0:1, 0:2], c.ident[0:1, 1:2],
                                    c.ones_bf[0:1, 0:2], start=False, stop=True,
                                    skip_group_check=True)
                kb.step("tensor", scmm, reads=[c.trk(qf), c.trk(kf)],
                        writes=[c.trk(sc)])
                kb.step("scalar", lambda e, sc=sc, pr=pr, h=h, L=L: e.activation(
                    pr[:, 0:L], sc[:, 0:L], AF.Exp,
                    bias=c.bias3[:, 32 + h:33 + h], scale=SCALE1),
                    reads=[c.trk(sc), c.trk(c.bias3)], writes=[c.trk(pr)])
                if w == nw - 1:
                    kb.step("gpsimd", lambda e, pr=pr, L=L: e.affine_select(
                        out=pr[:, L - 128:L], in_=pr[:, L - 128:L],
                        compare_op=ALU.is_ge, fill=0.0, base=0,
                        pattern=[[-1, 128]], channel_multiplier=1),
                        reads=[c.trk(pr)], writes=[c.trk(pr)])
                return par

            def emit_tr(h, w, par):
                L = min(QW, klen - w * QW)
                pr = probs[par]
                pT = probsT[par]
                ptp = [c.pt_a, c.pt_b][par % 2]
                nch = ceil_div(L, 128)
                def trgrp(e, pr=pr, ptp=ptp, nch=nch):
                    for tch in range(nch):
                        ins = e.transpose(ptp[:, tch * 128:(tch + 1) * 128],
                                          pr[:, tch * 128:(tch + 1) * 128], c.ident[:])
                    return ins
                kb.step("tensor", trgrp, reads=[c.trk(pr), c.trk(c.ident)],
                        writes=[c.trk(ptp)])
                kb.step("vector", lambda e, pT=pT, ptp=ptp, L=L: e.tensor_copy(
                    pT[:, 0:L], ptp[:, 0:L]), reads=[c.trk(ptp)], writes=[c.trk(pT)])

            def emit_pv(h, w, par):
                L = min(QW, klen - w * QW)
                kv = h // 4
                pT = probsT[par]
                nch = ceil_div(L, 128)
                yc = ycol1(h)
                def pvgrp(e, pT=pT, h=h, kv=kv, w=w, nch=nch, yc=yc,
                          first=(w == 0), last=(w == nw - 1)):
                    for tch in range(nch):
                        kc = 4 * w + tch
                        ins = e.matmul(c.ps_y[:, yc:yc + 129],
                                       pT[:, tch * 128:(tch + 1) * 128],
                                       Vtm[:, kc * 258 + 129 * kv: kc * 258 + 129 * (kv + 1)],
                                       start=(first and tch == 0),
                                       stop=(last and tch == nch - 1))
                    return ins
                kb.step("tensor", pvgrp, reads=[c.trk(pT), c.trk(Vtm)],
                        writes=[c.trk(c.ps_y)])

            plist = []
            for u in units:
                par = emit_sc(*u)
                plist.append((u[0], u[1], par))
                if len(plist) >= 2:
                    emit_tr(*plist[-2])
                if len(plist) >= 3:
                    emit_pv(*plist[-3])
            emit_tr(*plist[-1])
            if len(plist) >= 2:
                emit_pv(*plist[-2])
            emit_pv(*plist[-1])
            fl3 = c.pe_flush()
            for g, ng in ((0, 3), (1, 3), (2, 2)):
                kb.step("vector", lambda e, g=g, ng=ng: e.reciprocal(
                    c.recips[:, 3 * g:3 * g + ng],
                    bass.AP(c.ps_y, 512 * g + 128, [[1536, 128], [129, ng]])),
                    reads=[c.trk(c.ps_y)], writes=[c.trk(c.recips)], waits=[fl3])
            for h in range(H1):
                yc = ycol1(h)
                kb.step("scalar", lambda e, h=h, yc=yc: e.mul(
                    yn[:, 128 * h:128 * (h + 1)], c.ps_y[:, yc:yc + 128],
                    c.recips[:, h:h + 1]),
                    reads=[c.trk(c.ps_y), c.trk(c.recips)], writes=[c.trk(yn)])
            for half in range(2):
                ptp = [c.pt_a, c.pt_b][half]
                def ytr(e, half=half, ptp=ptp):
                    for c2 in range(4):
                        ins = e.transpose(ptp[:, c2 * 128:(c2 + 1) * 128],
                                          yn[:, (half * 4 + c2) * 128:(half * 4 + c2 + 1) * 128],
                                          c.ident[:])
                    return ins
                kb.step("tensor", ytr, reads=[c.trk(yn), c.trk(c.ident)], writes=[c.trk(ptp)])
                dstv = yfm[:].rearrange("p (c t) -> p c t", c=CH)[
                    :, half * 4:(half + 1) * 4, qq * 128:(qq + 1) * 128]
                kb.step("vector", lambda e, ptp=ptp, dstv=dstv: e.tensor_copy(
                    dstv, ptp[:].rearrange("p (c t) -> p c t", c=4)),
                    reads=[c.trk(ptp)], writes=[c.trk(yfm)])
        if q + 1 < NQ:
            emit_A1(q + 1)
        for m in range(CH):
            pb = rot[m % 3]
            mm_group(pb, slice(None), slice(0, QW),
                     [wo[:, m * 1024 + cc * 128: m * 1024 + (cc + 1) * 128]
                      for cc in range(CH)],
                     [yfm[:, cc * QW:(cc + 1) * QW] for cc in range(CH)],
                     [c.trk(yfm), c.trk(wo)], extra_waits=[kwtok])
            kb.step("vector", lambda e, m=m, pb=pb: e.tensor_copy(
                ao[:, m * QW:(m + 1) * QW], pb[:, 0:QW]),
                reads=[c.trk(pb)], writes=[c.trk(ao)])
        emit_tail(core, ao, c.trk(ao), xf, c.trk(xf), src_x, trk_srcx,
                  dst_x, trk_dstx, q, xfsem)


def emit_b1ffn(core, seq, src_x, trk_srcx, dst_x, trk_dstx, st):
    kb, c = core.kb, core
    SB = c.CBASE
    S = seq
    kern = c.at(f"f1k{S}", [128, CH * 1024], F32, SB)
    kq = c.at(f"f1kq{S}", [128, CH * 1024], BF16, SB + 32768)
    la = c.at(f"f1la{S}", [128, 128], BF16, SB + 49152)
    lb = c.at(f"f1lb{S}", [16, CH * 128], BF16, SB + 49408)
    biasb = c.at(f"f1bias{S}", [128, 8], F32, SB + 51456)
    qcols_t = c.at(f"f1qc{S}", [128, 16], F32, SB + 51488)
    xbf = [c.at(f"f1xbf{S}_{i}", [128, CH * QW], BF16, SB + 51552 + i * 8192)
           for i in range(2)]
    xa = c.at(f"f1xa{S}", [16, QW], BF16, SB + 67936)
    out_q = c.at(f"f1out{S}", [128, CH * QW], BF16, SB + 68960)
    outs = c.at(f"f1outs{S}", [128, CH * QW], BF16, SB + 77152)
    xf = c.at(f"f1xf{S}", [128, CH * 256], F32, SB + 85344)
    xlsem = [kb.sem(f"f1xl{S}_{i}") for i in range(2)]
    xfsem = kb.sem(f"f1xf{S}")
    kwsem = kb.sem(f"f1kw{S}")

    rot = [c.ps_a, c.ps_b, c.ps_c]

    srck = c.Wsrc["kern"]
    kb.step("sync", lambda e: e.dma_start(
        kern[:].rearrange("p (g f) -> p g f", g=CH),
        srck.rearrange("g p f -> p g f")), writes=[c.trk(kern)], sem=kwsem, amt=16)
    kb.step("gpsimd", lambda e: e.dma_start(la[:], c.Wsrc["la"][0]),
            writes=[c.trk(la)], sem=kwsem, amt=16)
    kb.step("gpsimd", lambda e: e.dma_start(
        lb[:].rearrange("p (g f) -> p g f", g=CH),
        c.Wsrc["lb"].rearrange("g p f -> p g f")), writes=[c.trk(lb)], sem=kwsem, amt=16)
    kb.step("gpsimd", lambda e: e.dma_start(biasb[:], c.Wsrc["bias1"]),
        writes=[c.trk(biasb)], sem=kwsem, amt=16)
    kwtok = (kwsem, kwsem.n)

    for m in range(CH):
        kb.step("vector", lambda e, m=m: e.reduce_max(
            c.bias3[:, m:m + 1], kern[:, m * 1024:(m + 1) * 1024], axis=AX.X,
            apply_absolute_value=True),
            reads=[c.trk(kern)], writes=[c.trk(c.bias3)], waits=[kwtok])
    kb.step("vector", lambda e: e.reduce_max(
        c.bias3[:, 8:9], c.bias3[:, 0:8], axis=AX.X),
        reads=[c.trk(c.bias3)], writes=[c.trk(c.bias3)])
    kb.step("tensor", lambda e: e.transpose(c.ps_c[0:1, 0:128],
            c.bias3[:, 8:9], c.identf[:]),
            reads=[c.trk(c.bias3), c.trk(c.identf)], writes=[c.trk(c.ps_c)])
    fl = c.pe_flush()
    kb.step("vector", lambda e: e.reduce_max(c.rows_c[0:1, 0:1], c.ps_c[0:1, 0:128],
            axis=AX.X), reads=[c.trk(c.ps_c)], writes=[c.trk(c.rows_c)], waits=[fl])
    kb.step("vector", lambda e: e.reciprocal(c.rows_c[0:1, 1:2], c.rows_c[0:1, 0:1]),
            reads=[c.trk(c.rows_c)], writes=[c.trk(c.rows_c)])
    kb.step("vector", lambda e: e.tensor_scalar(c.rows_c[0:1, 2:3], c.rows_c[0:1, 1:2],
            7.0, None, ALU.mult), reads=[c.trk(c.rows_c)], writes=[c.trk(c.rows_c)])
    kb.step("vector", lambda e: e.tensor_scalar(c.rows_c[0:1, 3:4], c.rows_c[0:1, 0:1],
            1.0 / 7.0, None, ALU.mult), reads=[c.trk(c.rows_c)], writes=[c.trk(c.rows_c)])
    kb.step("tensor", lambda e: e.matmul(c.ps_c[:, 16:18], c.onesf[0:1, :],
            c.rows_c[0:1, 2:4], start=True, stop=True),
            reads=[c.trk(c.rows_c), c.trk(c.onesf)], writes=[c.trk(c.ps_c)])
    fl2 = c.pe_flush()
    kb.step("vector", lambda e: e.tensor_copy(qcols_t[:, 0:2], c.ps_c[:, 16:18]),
            reads=[c.trk(c.ps_c)], writes=[c.trk(qcols_t)], waits=[fl2])
    for m in range(CH):
        sl = slice(m * 1024, (m + 1) * 1024)
        kb.step("vector", lambda e, sl=sl: e.tensor_scalar(
            kern[:, sl], kern[:, sl], qcols_t[:, 0:1], None, ALU.mult),
            reads=[c.trk(kern), c.trk(qcols_t)], writes=[c.trk(kern)])
        kb.step("vector", lambda e, sl=sl: e.tensor_scalar(
            kern[:, sl], kern[:, sl], MAGIC, MAGIC, ALU.add, ALU.subtract),
            reads=[c.trk(kern)], writes=[c.trk(kern)])
        kb.step("vector", lambda e, sl=sl: e.tensor_scalar(
            kern[:, sl], kern[:, sl], 7.0, None, ALU.min),
            reads=[c.trk(kern)], writes=[c.trk(kern)])
        kb.step("vector", lambda e, sl=sl: e.tensor_scalar(
            kq[:, sl], kern[:, sl], -8.0, None, ALU.max),
            reads=[c.trk(kern)], writes=[c.trk(kq)])

    for q in range(NQ):
        xb = xbf[q % 2]
        txb = c.trk(xb)
        srcv = src_x.rearrange("(c k) t -> k c t", k=128)[:, :, q * QW:(q + 1) * QW]
        kb.step("gpsimd", lambda e, d=xb, s=srcv: e.dma_start(
            d[:].rearrange("k (c t) -> k c t", c=CH), s),
            reads=[trk_srcx], writes=[txb], sem=xlsem[q % 2], amt=16)
        emit_ln(core, xb, txb)
        def xamm(e, xb=xb):
            for cc in range(CH):
                e.matmul(c.ps_c[0:16, 0:QW], la[:, cc * 16:(cc + 1) * 16],
                         xb[:, cc * QW:(cc + 1) * QW],
                         start=(cc == 0), stop=(cc == CH - 1))
            return e.matmul(c.ps_c[0:1, 508:510], c.ident[0:1, 1:2],
                            c.ones_bf[0:1, 0:2], start=True, stop=True,
                            skip_group_check=True)
        kb.step("tensor", xamm, reads=[txb, c.trk(la)], writes=[c.trk(c.ps_c)],
                waits=[kwtok])
        kb.step("vector", lambda e: e.tensor_copy(xa[:], c.ps_c[0:16, 0:QW]),
                reads=[c.trk(c.ps_c)], writes=[c.trk(xa)])
        for m in range(CH):
            pb = rot[m % 3]
            plor = c.ps_y
            def mainmm(e, m=m, pb=pb, xb=xb):
                for cc in range(CH):
                    ins = e.matmul(pb[:, 0:QW],
                                   kq[:, m * 1024 + cc * 128: m * 1024 + (cc + 1) * 128],
                                   xb[:, cc * QW:(cc + 1) * QW],
                                   start=(cc == 0), stop=(cc == CH - 1))
                return ins
            kb.step("tensor", mainmm, reads=[txb, c.trk(kq)], writes=[c.trk(pb)])
            def loramm(e, m=m, plor=plor):
                return e.matmul(plor[:, 0:QW], lb[:, m * 128:(m + 1) * 128], xa[:],
                                start=True, stop=True)
            kb.step("tensor", loramm, reads=[c.trk(xa), c.trk(lb)], writes=[c.trk(plor)])
            kb.step("vector", lambda e, m=m, pb=pb: e.tensor_scalar(
                outs[:, m * QW:(m + 1) * QW], pb[:, 0:QW], qcols_t[:, 1:2],
                biasb[:, m:m + 1], ALU.mult, ALU.add),
                reads=[c.trk(pb), c.trk(qcols_t), c.trk(biasb)], writes=[c.trk(outs)])
            kb.step("vector", lambda e, m=m, plor=plor: e.tensor_tensor(
                out_q[:, m * QW:(m + 1) * QW], outs[:, m * QW:(m + 1) * QW],
                plor[:, 0:QW], ALU.add),
                reads=[c.trk(outs), c.trk(plor)], writes=[c.trk(out_q)])
        emit_tail(core, out_q, c.trk(out_q), xf, c.trk(xf), src_x, trk_srcx,
                  dst_x, trk_dstx, q, xfsem)
